# version 06
# speedup vs baseline: 1.1421x; 1.0256x over previous
import numpy as np
import ml_dtypes
import concourse.bass as bass
import concourse.mybir as mybir
from concourse.bass_core_rust import IndirectOffsetOnAxis
from contextlib import ExitStack

F32 = mybir.dt.float32
BF16 = mybir.dt.bfloat16
I32 = mybir.dt.int32
AF = mybir.ActivationFunctionType
ALU = mybir.AluOpType
AX = mybir.AxisListType

T, D, H, HD, BLK = 8192, 1024, 16, 64, 32
NT = 23
NB = T // BLK
GRP = 64
CAP = 16
HID = 4096
NE, NSH = 8, 2
CE = 3072
EPS = 1e-5
BISECT_ITERS = 44

bf = ml_dtypes.bfloat16
ENGINES = ("sync", "gpsimd", "vector", "scalar", "tensor")


class Step:
    __slots__ = ("name", "engine", "fn", "reads", "writes", "deps", "idx", "count_after", "sem_incs", "dep_steps", "sem_key", "group")

    def __init__(self, name, engine, fn, reads, writes, deps):
        self.name, self.engine, self.fn = name, engine, fn
        self.reads, self.writes, self.deps = list(reads), list(writes), list(deps)


class Sched:
    def __init__(self, P=None):
        self.P = P
        self.steps = []
        self.by_name = {}

    def keys_of(self, b):
        if self.P is not None:
            t = self.P.tokens.get(id(b))
            if t is not None:
                return t
        return (id(b),)

    def add(self, name, engine, fn, reads=(), writes=(), deps=()):
        assert engine in ENGINES
        assert name not in self.by_name, name
        st = Step(name, engine, fn, reads, writes, deps)
        st.idx = len(self.steps)
        self.steps.append(st)
        self.by_name[name] = st
        return st

    def resolve(self):
        last_writer = {}
        readers = {}
        for st in self.steps:
            dep_steps = set()
            for d in st.deps:
                dep_steps.add(self.by_name[d] if isinstance(d, str) else d)
            rkeys = [k for b in st.reads for k in self.keys_of(b)]
            wkeys = [k for b in st.writes for k in self.keys_of(b)]
            for k in rkeys:
                w = last_writer.get(k)
                if w is not None:
                    dep_steps.add(w)
            for k in wkeys:
                w = last_writer.get(k)
                if w is not None:
                    dep_steps.add(w)
                for r in readers.get(k, ()):
                    dep_steps.add(r)
            for k in rkeys:
                readers.setdefault(k, []).append(st)
            for k in wkeys:
                last_writer[k] = st
                readers[k] = []
            st.dep_steps = [d for d in dep_steps if d is not st]
            for d in st.dep_steps:
                assert d.idx < st.idx, f"forward dep {d.name} -> {st.name}"

    def emit(self, P, block):
        self.resolve()
        semkeys = [e for e in ENGINES] + [e + "_dma" for e in ENGINES] + \
                  [e + "_g0" for e in ENGINES] + [e + "_g1" for e in ENGINES]
        sems = {k: P.sem("s_" + k) for k in semkeys}
        counts = {k: 0 for k in semkeys}
        gord = {e: 0 for e in ENGINES}
        gcum = {}
        for st in self.steps:
            st.sem_incs = None
            st.count_after = None
            st.group = None
        CAP = 8
        open_group = {}
        step_group_open = {}
        def close(e):
            if e in open_group:
                for m in open_group[e][1]:
                    step_group_open.pop(id(m), None)
                del open_group[e]
        for st in self.steps:
            for d in st.dep_steps:
                if id(d) in step_group_open:
                    close(step_group_open[id(d)])
            dma = getattr(st.fn, "dma", False)
            grouped = dma and getattr(st.fn, "grouped", False)
            if grouped:
                e = st.engine
                if e not in open_group:
                    open_group[e] = (gord[e], [])
                    gord[e] += 1
                g, members = open_group[e]
                st.group = g
                members.append(st)
                step_group_open[id(st)] = e
                if len(members) >= CAP:
                    close(e)
        for st in self.steps:
            dma = getattr(st.fn, "dma", False)
            n = getattr(st.fn, "n_incs", 1) * (16 if dma else 1)
            if st.group is not None:
                st.sem_key = st.engine + "_g%d" % (st.group % 2)
                counts[st.sem_key] += n
                st.count_after = counts[st.sem_key]
                gcum[(st.engine, st.group)] = st.count_after
            else:
                st.sem_key = st.engine + ("_dma" if dma else "")
                counts[st.sem_key] += n
                st.count_after = counts[st.sem_key]
        counts_final = dict(counts)
        P.sems = sems
        engmap = dict(sync=block.sync, gpsimd=block.gpsimd, vector=block.vector,
                      scalar=block.scalar, tensor=block.tensor)
        for ename in ENGINES:
            mysteps = [s for s in self.steps if s.engine == ename]
            if not mysteps:
                continue

            def mk(ename=ename, mysteps=mysteps):
                def body(eng):
                    waited = {}
                    LAG = 1
                    my_dma_hist = []
                    g_issued = {}
                    dkey = ename + "_dma"
                    for st in mysteps:
                        is_dma_st = getattr(st.fn, "dma", False)
                        is_grp = st.group is not None
                        if is_dma_st and not is_grp:
                            my_dma_hist.append(st.count_after)
                            if len(my_dma_hist) > LAG:
                                tgt = my_dma_hist[-LAG - 1]
                                if waited.get(dkey, -1) < tgt:
                                    eng.wait_ge(P.sems[dkey], tgt)
                                    waited[dkey] = tgt
                        if is_grp and st.group >= 2:
                            pg = (ename, st.group - 2)
                            if pg in gcum:
                                tgt = gcum[pg]
                                if waited.get(st.sem_key, -1) < tgt:
                                    eng.wait_ge(P.sems[st.sem_key], tgt)
                                    waited[st.sem_key] = tgt
                        for d in st.dep_steps:
                            key = d.sem_key
                            if d.group is not None:
                                tgt = gcum[(d.engine, d.group)]
                            else:
                                tgt = d.count_after
                            if waited.get(key, -1) >= tgt:
                                continue
                            eng.wait_ge(P.sems[key], tgt)
                            waited[key] = tgt
                        instrs = st.fn(eng)
                        if instrs is None:
                            instrs = []
                        if not isinstance(instrs, (list, tuple)):
                            instrs = [instrs]
                        if is_dma_st:
                            assert len(instrs) == getattr(st.fn, "n_incs", 1), st.name
                            for i in instrs:
                                i.then_inc(P.sems[st.sem_key], 16)
                        else:
                            assert len(instrs) >= 1, f"{st.name} returned no instr"
                            instrs[-1].then_inc(P.sems[ename], 1)
                    for key in (dkey, ename + "_g0", ename + "_g1"):
                        if counts_final[key] and any(
                                getattr(s2.fn, "dma", False) and s2.sem_key == key
                                for s2 in mysteps):
                            eng.wait_ge(P.sems[key], counts_final[key])
                return body
            engmap[ename](mk())
        return sems, counts


def dma_step(n=1, grouped=False):
    def wrap(fn):
        fn.dma = True
        fn.n_incs = n
        fn.grouped = grouped
        return fn
    return wrap

GROUP_SZ = 2


def build_consts():
    c = {}
    ii = np.arange(128)
    c["ident"] = np.eye(128, dtype=bf)
    c["ones_bf"] = np.ones((128, 512), dtype=bf)
    c["ones_f32"] = np.ones((128, 4), dtype=np.float32)
    band = (np.abs(np.arange(NB)[:, None] - np.arange(NB)[None, :]) <= 1).astype(np.float32) * 5000.0
    c["band0"] = band[0:128].copy()
    c["band1"] = band[128:256].copy()
    c["ltinc"] = (ii[:, None] <= ii[None, :]).astype(bf)
    c["ltexcl"] = (ii[:, None] < ii[None, :]).astype(bf)
    c["blk4"] = (ii[:, None] // 32 == np.arange(4)[None, :]).astype(bf)
    c["grpind"] = (ii[:, None] // 8 == np.arange(16)[None, :]).astype(np.float32)
    c["expind"] = (np.arange(16)[:, None] == ii[None, :] // 8).astype(np.float32)
    e16 = np.zeros((16, 16 * 128), dtype=bf)
    for h in range(16):
        e16[h, 128 * h:128 * (h + 1)] = 1
    c["exp16"] = e16
    c["siota1"] = np.tile(np.arange(CAP, dtype=np.float32), GRP)[None, :]
    c["siota128"] = np.tile(np.tile(np.arange(CAP, dtype=np.float32), GRP)[None, :], (128, 1))
    c["kbiota"] = np.stack([ii.astype(np.float32), (ii + 128).astype(np.float32)], 1)
    c["neg1e9row"] = np.full((1, 128), -1e9, dtype=bf)
    x = np.arange(4224)
    c["pattc"] = (x[None, :] // 32 == ii[:, None]).astype(bf)
    c["eiota"] = np.tile(np.arange(8, dtype=np.float32)[None, :], (128, 1))
    c["epscol"] = np.full((128, 1), 1e-5, dtype=np.float32)
    c["ltexclf"] = (ii[:, None] < ii[None, :]).astype(np.float32)
    c["tokiota"] = ii.astype(np.float32)[:, None]
    c["identf"] = np.eye(128, dtype=np.float32)
    c["kbase8"] = (128 * np.arange(8)[None, :] + ii[:, None]).astype(np.float32)
    c["qbase4"] = (128 * np.arange(4)[None, :] + ii[:, None]).astype(np.float32)
    c["tiota32"] = np.tile(np.arange(32, dtype=np.float32)[None, :], (128, 1))
    c["zeros128"] = np.zeros((128, 1024), dtype=np.float32)
    cm = np.zeros((16, 4, 4, 32), dtype=bf)
    for s_ in range(16):
        cm[s_, s_ // 4, :, :] = 1
    c["cmask16"] = cm.reshape(16, 512)
    p4 = np.zeros((16, 128), dtype=bf)
    for s_ in range(16):
        p4[s_, 32 * (s_ % 4):32 * (s_ % 4 + 1)] = 1
    c["patt4x"] = p4
    c["onesfr"] = np.ones((1, 128), dtype=np.float32)
    c["lnepscol"] = np.full((128, 1), np.log(1e-6), dtype=np.float32)
    c["kbiotab"] = np.stack([ii.astype(bf), (ii + 128).astype(bf)], 1)
    return c


class P:
    def __init__(self, debug_outputs=()):
        self.debug_outputs = set(debug_outputs)
        self.nc = bass.Bass("TRN2", target_bir_lowering=False)
        self.ctx = ExitStack()
        self.dram = {}
        self.consts_np = build_consts()
        self.sched = Sched(self)
        self.tokens = {}
        self.pools = {}

    GRAN = 1024

    def mkpool(self, name, n_elems, dtype):
        t = self.ctx.enter_context(self.nc.sbuf_tensor("pool_" + name, [128, n_elems], dtype))
        self.pools[name] = [t, 0, n_elems]
        return t

    def preset(self, name):
        self.pools[name][1] = 0

    def pmark(self, name):
        return self.pools[name][1]

    def preset_to(self, name, mark):
        self.pools[name][1] = mark

    def alias(self, view, of):
        self.tokens[id(view)] = self.tokens.get(id(of), (id(of),))
        return view

    def palloc(self, pool, shape, name=""):
        t, ptr, cap = self.pools[pool]
        n = 1
        for d in shape[1:]:
            n *= d
        start = ptr
        end = ptr + n
        assert end <= cap, f"pool {pool} overflow at {name}: {end} > {cap}"
        self.pools[pool][1] = end
        v = t[0:shape[0], start:end]
        if len(shape) > 2:
            if len(shape) == 3:
                v = v.rearrange("p (a b) -> p a b", a=shape[1], b=shape[2])
            else:
                raise NotImplementedError
        g0, g1 = start // self.GRAN, (end - 1) // self.GRAN
        self.tokens[id(v)] = tuple(f"{pool}:{g}" for g in range(g0, g1 + 1))
        return v

    def sem(self, name):
        return self.ctx.enter_context(self.nc.semaphore(name))

    def sb(self, name, shape, dtype):
        return self.ctx.enter_context(self.nc.sbuf_tensor(name, shape, dtype))

    def ps(self, name, shape, dtype=F32):
        return self.ctx.enter_context(self.nc.psum_tensor(name, shape, dtype))

    def din(self, name, shape, dtype):
        t = self.nc.dram_tensor(name, shape, dtype, kind="ExternalInput")
        self.dram[name] = t
        return t.ap()

    def dout(self, name, shape, dtype):
        t = self.nc.dram_tensor(name, shape, dtype, kind="ExternalOutput")
        self.dram[name] = t
        return t.ap()

    def dint(self, name, shape, dtype):
        kind = "ExternalOutput" if name in self.debug_outputs else "Internal"
        t = self.nc.dram_tensor(name, shape, dtype, kind=kind)
        self.dram[name] = t
        return t.ap()

    def const_inputs(self):
        return {("c_" + k): v for k, v in self.consts_np.items()}

    def declare_common(self):
        self.mkpool("W", 24576, BF16)
        self.mkpool("BF", 37888, BF16)
        self.mkpool("F32", 16384, F32)
        self.cst = {}
        for k, v in self.consts_np.items():
            dt = {np.dtype(np.float32): F32, np.dtype(bf): BF16, np.dtype(np.int32): I32}[v.dtype]
            self.cst[k] = self.din("c_" + k, list(v.shape), dt)
        pA = self.ps("pA", [128, 512], F32)
        pB = self.ps("pB", [128, 512], F32)
        pC = self.ps("pC", [128, 512], F32)
        pD = self.ps("pD", [128, 512], F32)
        pT1 = self.ps("pT1", [128, 512], BF16)
        pT2 = self.ps("pT2", [128, 512], BF16)
        pE = self.ps("pE", [128, 512], F32)
        self.psum = dict(A=pA, B=pB, C=pC, D=pD, T1=pT1, T2=pT2, E=pE)
        self.ident = self.sb("identc", [128, 128], BF16)
        self.epscol = self.sb("epscol", [128, 1], F32)
        self.onesrow = self.sb("onesrow", [1, 512], BF16)
        self.onesfr = self.sb("onesfr", [1, 128], F32)
        self.identf = self.sb("identf", [128, 128], F32)
        self.onesbb = self.sb("onesbb", [128, 128], BF16)
        self.b1stg = self.sb("b1stg", [32, 128], F32)
        self.bkrow = self.sb("bkrow", [1, 1024], BF16)
        self.bvrow = self.sb("bvrow", [1, 1024], BF16)

    def emit_consts(self):
        S = self.sched
        cst = self.cst

        @dma_step(6)
        def cc_load(eng):
            return [eng.dma_start(out=self.ident[:], in_=cst["ident"]),
                    eng.dma_start(out=self.onesrow[:], in_=cst["ones_bf"][0:1, 0:512]),
                    eng.dma_start(out=self.epscol[:], in_=cst["epscol"]),
                    eng.dma_start(out=self.onesfr[:], in_=cst["onesfr"]),
                    eng.dma_start(out=self.identf[:], in_=cst["identf"]),
                    eng.dma_start(out=self.onesbb[:], in_=cst["ones_bf"][:, 0:128])]
        S.add("cc_load", "sync", cc_load,
              writes=[self.ident, self.onesrow, self.epscol, self.onesfr, self.identf, self.onesbb])

    def declare_p1(self):
        self.X = self.din("X", [T, D], F32)
        self.Wq = self.din("Wq", [D, D], F32)
        self.Wk = self.din("Wk", [D, D], F32)
        self.Wv = self.din("Wv", [D, D], F32)
        self.bq = self.din("bq", [D], F32)
        self.bk = self.din("bk", [D], F32)
        self.bv = self.din("bv", [D], F32)
        self.QT = self.dint("QT", [D, T], BF16)
        self.KH = self.dint("KH", [H, T, HD], BF16)
        self.VH = self.dint("VH", [H, T, HD], BF16)
        self.XNHT = self.dint("XNHT", [128, 8, NB], F32)
        self.WQT = self.palloc("W", [128, 8, 1024], "WQT")
        self.WKT = self.palloc("W", [128, 8, 1024], "WKT")
        self.WVT = self.palloc("W", [128, 8, 1024], "WVT")
        self.xnht = self.palloc("F32", [128, 8, NB], "xnht")
        self.bqt = self.sb("bqt", [128, 8], F32)
        self.bkt = self.sb("bkt", [128, 8], F32)

    def emit_p1(self):
        S = self.sched
        cst = self.cst
        nc = self.nc
        Wr = [self.palloc("BF", [128, 8, 1024], "Wr0")]
        bkrow, bvrow = self.bkrow, self.bvrow
        onesrow = self.onesrow
        Xt = [self.palloc("F32", [128, 4, 1024], f"Xt{i}") for i in range(2)]
        xnb = [self.palloc("BF", [128, 4, 1024], f"xnb{i}") for i in range(2)]
        xnT = [self.palloc("BF", [128, 8, 512], f"xnT{i}") for i in range(2)]
        qout = [self.palloc("BF", [128, 8, 512], f"qout{i}") for i in range(2)]
        kvout = [self.palloc("BF", [128, 2, 1024], f"kvout{i}") for i in range(2)]
        stat = [self.sb(f"stat{i}", [128, 24], F32) for i in range(2)]
        junk = self.palloc("BF", [128, 1024], "junk")
        pA, pB, pC, pD, pT1, pT2, pE = (self.psum[k] for k in ("A", "B", "C", "D", "T1", "T2", "E"))

        X, QT, KH, VH = self.X, self.QT, self.KH, self.VH
        WQT, WKT, WVT, xnht, bqt, ident = self.WQT, self.WKT, self.WVT, self.xnht, self.bqt, self.ident

        epscol = self.epscol

        @dma_step(16)
        def b_load(eng):
            ins = []
            for c in range(8):
                ins.append(eng.dma_start(out=bqt[:, c:c + 1],
                                         in_=self.bq[128 * c:128 * (c + 1)].rearrange("(a b) -> a b", b=1)))
                ins.append(eng.dma_start(out=self.bkt[:, c:c + 1],
                                         in_=self.bk[128 * c:128 * (c + 1)].rearrange("(a b) -> a b", b=1)))
            return ins
        S.add("b_load", "sync", b_load, writes=[bqt, self.bkt])

        @dma_step(2)
        def bkv_load(eng):
            return [eng.dma_start(out=bkrow[:], in_=self.bk.rearrange("(b a) -> b a", b=1)),
                    eng.dma_start(out=bvrow[:], in_=self.bv.rearrange("(b a) -> b a", b=1))]
        S.add("bkv_load", "gpsimd", bkv_load, writes=[bkrow, bvrow])

        def bq_scale(eng):
            return eng.tensor_scalar(bqt[:], bqt[:], 0.125, None, op0=ALU.mult)
        S.add("bq_scale", "vector", bq_scale, reads=[bqt], writes=[bqt])

        for j, (Wsrc, nm, WT) in enumerate(((self.Wq, "q", WQT), (self.Wk, "k", WKT), (self.Wv, "v", WVT))):
            scale = 0.125 if nm == "q" else 1.0

            @dma_step(8)
            def w_load(eng, W=Wsrc):
                return [eng.dma_start(out=Wr[0][:, mc, :], in_=W[128 * mc:128 * (mc + 1), :])
                        for mc in range(8)]
            S.add(f"wld_{nm}", "gpsimd", w_load, writes=[Wr[0]])
            for g4 in range(16):
                mc, kc4 = g4 // 2, g4 % 2
                ptr = (pT1, pT2)[g4 % 2]

                def wt(eng, j=j, mc=mc, kc4=kc4, ptr=ptr):
                    ii = None
                    for a in range(4):
                        kc = kc4 * 4 + a
                        ii = eng.transpose(ptr[:, 128 * a:128 * (a + 1)],
                                           Wr[0][:, mc, 128 * kc:128 * (kc + 1)], ident[:])
                    return ii
                S.add(f"wt_{nm}_{g4}", "tensor", wt, reads=[Wr[0], ident], writes=[ptr])

                def wev(eng, WT=WT, mc=mc, kc4=kc4, ptr=ptr, scale=scale):
                    dst = WT[:, 4 * kc4:4 * (kc4 + 1), 128 * mc:128 * (mc + 1)]
                    return eng.activation(dst, ptr[:].rearrange("p (a q) -> p a q", a=4),
                                          AF.Copy, bias=0.0, scale=scale)
                S.add(f"wev_{nm}_{g4}", "scalar", wev, reads=[ptr], writes=[WT])

        NTI = 16
        for i in range(NTI):
            bi = i % 2
            xt, xb, xT, qo, kv, stt = Xt[bi], xnb[bi], xnT[bi], qout[bi], kvout[bi], stat[bi]

            @dma_step(1, grouped=True)
            def x_ld(eng, i=i, xt=xt):
                return eng.dma_start(out=xt[:],
                                     in_=X[512 * i:512 * (i + 1), :].rearrange("(s p) d -> p s d", p=128))
            S.add(f"x_ld_{i}", "sync", x_ld, writes=[xt])

            for s in range(4):
                def v_red(eng, xt=xt, stt=stt, s=s):
                    return eng.tensor_reduce(stt[:, 6 * s:6 * s + 1], xt[:, s, :], axis=AX.X, op=ALU.add)
                S.add(f"lnred_{i}_{s}", "vector", v_red, reads=[xt], writes=[stt])

                def s_sq(eng, xt=xt, stt=stt, s=s):
                    return eng.activation(junk[:], xt[:, s, :], AF.Square, bias=0.0, scale=1.0,
                                          accum_out=stt[:, 6 * s + 1:6 * s + 2])
                S.add(f"lnsq_{i}_{s}", "scalar", s_sq, reads=[xt], writes=[stt, junk])

                def v_mean(eng, stt=stt, s=s):
                    o = 6 * s
                    return eng.tensor_scalar(stt[:, o + 2:o + 3], stt[:, o:o + 1], 1.0 / D, None, op0=ALU.mult)
                S.add(f"lnmean_{i}_{s}", "vector", v_mean, reads=[stt], writes=[stt])

                def v_msq(eng, stt=stt, s=s):
                    o = 6 * s
                    return eng.tensor_tensor(stt[:, o + 3:o + 4], stt[:, o + 2:o + 3], stt[:, o + 2:o + 3], ALU.mult)
                S.add(f"lnmsq_{i}_{s}", "vector", v_msq, reads=[stt], writes=[stt])

                def v_var(eng, stt=stt, s=s):
                    o = 6 * s
                    return eng.tensor_scalar(stt[:, o + 4:o + 5], stt[:, o + 1:o + 2], 1.0 / D,
                                             stt[:, o + 3:o + 4], op0=ALU.mult, op1=ALU.subtract)
                S.add(f"lnvar_{i}_{s}", "vector", v_var, reads=[stt], writes=[stt])

                def s_sqrt(eng, stt=stt, s=s):
                    o = 6 * s
                    return eng.activation(stt[:, o + 5:o + 6], stt[:, o + 4:o + 5], AF.Sqrt,
                                          bias=epscol[:], scale=1.0)
                S.add(f"lnsqrt_{i}_{s}", "scalar", s_sqrt, reads=[stt, epscol], writes=[stt])

                def v_recip(eng, stt=stt, s=s):
                    o = 6 * s
                    return eng.reciprocal(stt[:, o + 5:o + 6], stt[:, o + 5:o + 6])
                S.add(f"lnrecip_{i}_{s}", "vector", v_recip, reads=[stt], writes=[stt])

                def v_norm(eng, xt=xt, xb=xb, stt=stt, s=s):
                    o = 6 * s
                    return eng.tensor_scalar(xb[:, s, :], xt[:, s, :], stt[:, o + 2:o + 3],
                                             stt[:, o + 5:o + 6], op0=ALU.subtract, op1=ALU.mult)
                S.add(f"lnnorm_{i}_{s}", "vector", v_norm, reads=[xt, stt], writes=[xb])

                for g4 in range(2):
                    ptr = (pT1, pT2)[g4]

                    def xtr(eng, xb=xb, s=s, g4=g4, ptr=ptr):
                        ii = None
                        for a in range(4):
                            fc = 4 * g4 + a
                            ii = eng.transpose(ptr[:, 128 * a:128 * (a + 1)],
                                               xb[:, s, 128 * fc:128 * (fc + 1)], ident[:])
                        return ii
                    S.add(f"xtr_{i}_{s}_{g4}", "tensor", xtr, reads=[xb, ident], writes=[ptr])

                    def xev(eng, xT=xT, s=s, g4=g4, ptr=ptr):
                        dst = xT[:, 4 * g4:4 * (g4 + 1), 128 * s:128 * (s + 1)]
                        return eng.activation(dst, ptr[:].rearrange("p (a q) -> p a q", a=4),
                                              AF.Copy, bias=0.0, scale=1.0)
                    S.add(f"xev_{i}_{s}_{g4}", "scalar", xev, reads=[ptr], writes=[xT])

            def xnh_red(eng, xT=xT, i=i):
                ii = None
                for fc in range(8):
                    ii = eng.tensor_reduce(xnht[:, fc, 16 * i:16 * (i + 1)],
                                           xT[:, fc, :].rearrange("p (n t) -> p n t", n=16),
                                           axis=AX.X, op=ALU.add)
                return ii
            S.add(f"xnh_{i}", "vector", xnh_red, reads=[xT], writes=[xnht])

            for mc in range(8):
                pq = (pA, pB)[mc % 2]

                def qmm(eng, xT=xT, mc=mc, pq=pq):
                    ii = None
                    for kc in range(8):
                        ii = eng.matmul(pq[:], WQT[:, kc, 128 * mc:128 * (mc + 1)], xT[:, kc, :],
                                        start=(kc == 0), stop=(kc == 7))
                    return ii
                S.add(f"qmm_{i}_{mc}", "tensor", qmm, reads=[WQT, xT], writes=[pq])

                def qev(eng, qo=qo, mc=mc, pq=pq):
                    return eng.tensor_scalar(qo[:, mc, :], pq[:], bqt[:, mc:mc + 1], None, op0=ALU.add)
                S.add(f"qev_{i}_{mc}", "vector", qev, reads=[pq, bqt], writes=[qo])

            @dma_step(8, grouped=True)
            def qt_st(eng, qo=qo, i=i):
                return [eng.dma_start(out=QT[128 * mc:128 * (mc + 1), 512 * i:512 * (i + 1)],
                                      in_=qo[:, mc, :]) for mc in range(8)]
            S.add(f"qt_st_{i}", "sync", qt_st, reads=[qo], writes=[QT])

            for s in range(4):
                def kvmm(eng, xT=xT, s=s):
                    ii = None
                    for kc in range(8):
                        st_ = (kc == 0)
                        sp_ = (kc == 7)
                        lhs = xT[:, kc, 128 * s:128 * (s + 1)]
                        eng.matmul(pA[:], lhs, WKT[:, kc, 0:512], start=st_, stop=False)
                        eng.matmul(pB[:], lhs, WKT[:, kc, 512:1024], start=st_, stop=False)
                        eng.matmul(pC[:], lhs, WVT[:, kc, 0:512], start=st_, stop=False)
                        ii = eng.matmul(pD[:], lhs, WVT[:, kc, 512:1024], start=st_, stop=False)
                    eng.matmul(pA[:], onesrow[0:1, 0:128], bkrow[0:1, 0:512], start=False, stop=True)
                    eng.matmul(pB[:], onesrow[0:1, 0:128], bkrow[0:1, 512:1024], start=False, stop=True)
                    eng.matmul(pC[:], onesrow[0:1, 0:128], bvrow[0:1, 0:512], start=False, stop=True)
                    return eng.matmul(pD[:], onesrow[0:1, 0:128], bvrow[0:1, 512:1024], start=False, stop=True)
                S.add(f"kvmm_{i}_{s}", "tensor", kvmm,
                      reads=[xT, WKT, WVT, onesrow, bkrow, bvrow], writes=[pA, pB, pC, pD])

                for pj, (pp, half, col) in enumerate(((pA, 0, 0), (pB, 0, 512), (pC, 1, 0), (pD, 1, 512))):
                    def kvev(eng, kv=kv, pp=pp, half=half, col=col):
                        return eng.activation(kv[:, half, col:col + 512], pp[:], AF.Copy, bias=0.0, scale=1.0)
                    S.add(f"kvev_{i}_{s}_{pj}", "scalar", kvev, reads=[pp], writes=[kv])

                @dma_step(16, grouped=True)
                def k_st(eng, kv=kv, i=i, s=s):
                    r0 = 512 * i + 128 * s
                    return [eng.dma_start(out=KH[h, r0:r0 + 128, :], in_=kv[:, 0, 64 * h:64 * (h + 1)])
                            for h in range(16)]
                S.add(f"k_st_{i}_{s}", "sync", k_st, reads=[kv], writes=[KH])

                @dma_step(16, grouped=True)
                def v_st(eng, kv=kv, i=i, s=s):
                    r0 = 512 * i + 128 * s
                    return [eng.dma_start(out=VH[h, r0:r0 + 128, :], in_=kv[:, 1, 64 * h:64 * (h + 1)])
                            for h in range(16)]
                S.add(f"v_st_{i}_{s}", "scalar", v_st, reads=[kv], writes=[VH])

        @dma_step(1)
        def xnh_st(eng):
            return eng.dma_start(out=self.XNHT, in_=xnht[:])
        S.add("xnh_st", "sync", xnh_st, reads=[xnht], writes=[self.XNHT])

    def declare_p2(self):
        self.ATTNO = self.dint("ATTNO", [H, T, HD], BF16)
        self.VSTG = self.dint("VSTG", [2, 1024, 2048], BF16)
        self.VMH = self.dint("VMH", [H, 128, 132], BF16)

    def emit_p2_sel(self):
        self.preset("BF")
        self.preset("F32")
        S = self.sched
        cst = self.cst
        WQT, WKT, WVT, ident = self.WQT, self.WKT, self.WVT, self.ident
        pA, pB, pC, pD, pT1, pT2, pE = (self.psum[k] for k in ("A", "B", "C", "D", "T1", "T2", "E"))

        threshall = self.sb("threshall", [128, 16], F32)
        lorow = self.sb("lorow", [1, 16], F32)
        acc = self.sb("acc", [128, 1], F32)
        bis = self.sb("bis", [16, 8], F32)
        bisI = self.sb("bisI", [16, 2], I32)
        midcol = self.sb("midcol", [128, 1], F32)
        rmxc = self.sb("rmxc", [128, 4], F32)
        lowTall = self.palloc("BF", [128, 16, 512], "lowTall")
        grpind = self.sb("grpind", [128, 16], F32)
        expind = self.sb("expind", [16, 128], F32)
        identf = self.identf
        band0 = self.palloc("F32", [128, 256], "band0")
        band1 = self.palloc("F32", [128, 256], "band1")
        ltinc = self.sb("ltinc", [128, 128], BF16)
        onesbb = self.onesbb
        siota1 = self.palloc("F32", [1, 1024], "siota1")
        siota128 = self.palloc("F32", [128, 1024], "siota128")
        kbiota = self.sb("kbiota", [128, 2], BF16)
        neg1e9row = self.sb("neg1e9row", [1, 128], BF16)
        pattc = self.palloc("BF", [128, 4224], "pattc")
        rowrow = self.palloc("BF", [1, 16 * 256], "rowrow")
        normall = self.palloc("F32", [128, 8192], "normall")
        self._bf_mark = self.pmark("BF")
        self._f32_mark = self.pmark("F32")
        nrmtmp = self.palloc("F32", [128, 512], "nrmtmp")
        cmpjunk = self.palloc("BF", [128, 8192], "cmpjunk")
        xnhtb = self.palloc("BF", [128, 8, NB], "xnhtb")
        lnepscol = self.sb("lnepscol", [128, 1], F32)
        onesfr = self.onesfr
        onesf = self.sb("onesf", [128, 4], F32)
        Vmt = self.sb("Vmt", [128, 2, 66], BF16)

        QT, KH, VH, ATTNO, VSTG = self.QT, self.KH, self.VH, self.ATTNO, self.VSTG
        xnht, bqt = self.xnht, self.bqt

        xnhtf = self.palloc("F32", [128, 8, NB], "xnhtf")
        wqs = self.palloc("F32", [64, 1024], "wqs")
        wks = self.palloc("F32", [64, 1024], "wks")
        wqTf = self.palloc("F32", [128, 8, 64], "wqTf")
        wkTf = self.palloc("F32", [128, 8, 64], "wkTf")
        QmTf = self.sb("QmTf", [64, NB], F32)
        KmTf = self.sb("KmTf", [64, NB], F32)

        def xnhtb_mk(eng):
            return eng.tensor_scalar(xnhtb[:], xnht[:], 1.0 / 32.0, None, op0=ALU.mult)
        S.add("xnhtb_mk", "vector", xnhtb_mk, reads=[xnht], writes=[xnhtb])

        def xnhtf_mk(eng):
            return eng.tensor_scalar(xnhtf[:], xnht[:], 1.0 / 32.0, None, op0=ALU.mult)
        S.add("xnhtf_mk", "vector", xnhtf_mk, reads=[xnht], writes=[xnhtf])

        @dma_step(12)
        def c2_load(eng):
            return [
                eng.dma_start(out=grpind[:], in_=cst["grpind"]),
                eng.dma_start(out=expind[:], in_=cst["expind"]),
                eng.dma_start(out=band0[:], in_=cst["band0"]),
                eng.dma_start(out=band1[:], in_=cst["band1"]),
                eng.dma_start(out=ltinc[:], in_=cst["ltinc"]),
                eng.dma_start(out=siota1[:], in_=cst["siota1"]),
                eng.dma_start(out=siota128[:], in_=cst["siota128"]),
                eng.dma_start(out=neg1e9row[:], in_=cst["neg1e9row"]),
                eng.dma_start(out=pattc[:], in_=cst["pattc"]),
                eng.dma_start(out=onesf[:], in_=cst["ones_f32"]),
                eng.dma_start(out=lnepscol[:], in_=cst["lnepscol"]),
                eng.dma_start(out=kbiota[:], in_=cst["kbiotab"]),
            ]
        S.add("c2_load", "sync", c2_load,
              writes=[grpind, expind, band0, band1, ltinc, siota1,
                      siota128, neg1e9row, pattc, onesf, lnepscol, kbiota])

        def vmt_z(eng):
            return eng.memset(Vmt[:], 0.0)
        S.add("vmt_z", "vector", vmt_z, writes=[Vmt])

        def vmt_ones(eng):
            return eng.memset(Vmt[:, :, 64:65], 1.0)
        S.add("vmt_ones", "vector", vmt_ones, reads=[Vmt], writes=[Vmt])

        def bis_init(eng):
            eng.memset(bis[:, 0:1], -70000.0)
            eng.memset(bis[:, 1:2], 70000.0)
            return eng.memset(bis[:, 2:3], 0.0)
        S.add("bis_init", "vector", bis_init, writes=[bis])

        bkrow2 = self.bkrow
        bvrow2 = self.bvrow
        onesrow2 = self.onesrow

        for h in range(H):
            @dma_step(2)
            def wqk_ld(eng, h=h):
                return [eng.dma_start(out=wqs[:], in_=self.Wq[64 * h:64 * (h + 1), :]),
                        eng.dma_start(out=wks[:], in_=self.Wk[64 * h:64 * (h + 1), :])]
            S.add(f"wqk_{h}", "sync", wqk_ld, reads=[self.Wq, self.Wk], writes=[wqs, wks])

            for kc8 in range(2):
                def wtr(eng, h=h, kc8=kc8):
                    i1 = None
                    for a in range(4):
                        kc = 4 * kc8 + a
                        eng.transpose(pC[:, 64 * a:64 * (a + 1)], wqs[:, 128 * kc:128 * (kc + 1)],
                                      identf[0:64, 0:64])
                        i1 = eng.transpose(pD[:, 64 * a:64 * (a + 1)], wks[:, 128 * kc:128 * (kc + 1)],
                                           identf[0:64, 0:64])
                    return i1
                S.add(f"wtr_{h}_{kc8}", "tensor", wtr, reads=[wqs, wks, identf], writes=[pC, pD])

                def wtre(eng, h=h, kc8=kc8):
                    eng.tensor_scalar(wqTf[:, 4 * kc8:4 * (kc8 + 1), :], pC[:, 0:256], 0.125, None,
                                      op0=ALU.mult)
                    return eng.tensor_copy(wkTf[:, 4 * kc8:4 * (kc8 + 1), :], pD[:, 0:256])
                S.add(f"wtre_{h}_{kc8}", "vector", wtre, reads=[pC, pD], writes=[wqTf, wkTf])

            def qm_mm(eng, h=h):
                i1 = None
                for kc in range(8):
                    i1 = eng.matmul(pE[0:64, 0:256], wqTf[:, kc, :], xnhtf[:, kc, :],
                                    start=(kc == 0), stop=(kc == 7))
                return i1
            S.add(f"qm_mm_{h}", "tensor", qm_mm, reads=[wqTf, xnhtf], writes=[pE])

            def qm_ev(eng, h=h):
                return eng.tensor_scalar(QmTf[:], pE[0:64, 0:256],
                                         bqt[64 * (h % 2):64 * (h % 2) + 64, h // 2:h // 2 + 1],
                                         None, op0=ALU.add)
            S.add(f"qm_ev_{h}", "vector", qm_ev, reads=[pE, bqt], writes=[QmTf])

            def km_mm(eng, h=h):
                i1 = None
                for kc in range(8):
                    i1 = eng.matmul(pE[0:64, 256:512], wkTf[:, kc, :], xnhtf[:, kc, :],
                                    start=(kc == 0), stop=(kc == 7))
                return i1
            S.add(f"km_mm_{h}", "tensor", km_mm, reads=[wkTf, xnhtf], writes=[pE])

            def km_ev(eng, h=h):
                return eng.tensor_scalar(KmTf[:], pE[0:64, 256:512],
                                         self.bkt[64 * (h % 2):64 * (h % 2) + 64, h // 2:h // 2 + 1],
                                         None, op0=ALU.add)
            S.add(f"km_ev_{h}", "vector", km_ev, reads=[pE, self.bkt], writes=[KmTf])

            def vm_mm(eng, h=h):
                last = None
                for nc2 in range(2):
                    for kc in range(8):
                        eng.matmul(pD[:, 66 * nc2:66 * nc2 + 64],
                                   xnhtb[:, kc, 128 * nc2:128 * (nc2 + 1)],
                                   WVT[:, kc, 64 * h:64 * (h + 1)], start=(kc == 0), stop=False)
                    last = eng.matmul(pD[:, 66 * nc2:66 * nc2 + 64], onesrow2[0:1, 0:128],
                                      bvrow2[0:1, 64 * h:64 * (h + 1)], start=False, stop=True)
                return last
            S.add(f"vm_mm_{h}", "tensor", vm_mm, reads=[xnhtb, WVT, onesrow2, bvrow2], writes=[pD])

            def vm_ev(eng, h=h):
                i1 = None
                for nc2 in range(2):
                    i1 = eng.activation(Vmt[:, nc2, 0:64], pD[:, 66 * nc2:66 * nc2 + 64],
                                        AF.Copy, bias=0.0, scale=1.0)
                return i1
            S.add(f"vm_ev_{h}", "scalar", vm_ev, reads=[pD], writes=[Vmt])

            @dma_step(1)
            def vm_st(eng, h=h):
                return eng.dma_start(out=self.VMH[h], in_=Vmt[:].rearrange("p a b -> p (a b)"))
            S.add(f"vm_st_{h}", "scalar", vm_st, reads=[Vmt], writes=[self.VMH])

            def low_mm(eng, h=h):
                for qc, pp in ((0, pA), (1, pB)):
                    eng.matmul(pp[:, 0:256], QmTf[:, 128 * qc:128 * (qc + 1)], KmTf[:],
                               start=True, stop=True)
                eng.matmul(pC[:, 0:256], KmTf[:, 0:128], QmTf[:], start=True, stop=True)
                return eng.matmul(pC[:, 256:512], KmTf[:, 128:256], QmTf[:], start=True, stop=True)
            S.add(f"low_mm_{h}", "tensor", low_mm, reads=[QmTf, KmTf], writes=[pA, pB, pC])

            def lowt_ev(eng, h=h):
                return eng.tensor_copy(lowTall[:, h, :], pC[:, 0:512])
            S.add(f"lowt_ev_{h}", "vector", lowt_ev, reads=[pC], writes=[lowTall])

            def rmx(eng, h=h):
                eng.tensor_reduce(rmxc[:, 0:1], pA[:, 0:256], axis=AX.X, op=ALU.max)
                return eng.tensor_reduce(rmxc[:, 1:2], pB[:, 0:256], axis=AX.X, op=ALU.max)
            S.add(f"rmx_{h}", "vector", rmx, reads=[pA, pB], writes=[rmxc])

            def rmrow_mm(eng, h=h):
                eng.matmul(pE[0:1, 0:128], rmxc[:, 0:1], identf[:], start=True, stop=True)
                return eng.matmul(pE[0:1, 128:256], rmxc[:, 1:2], identf[:], start=True, stop=True)
            S.add(f"rmrow_mm_{h}", "tensor", rmrow_mm, reads=[rmxc, identf], writes=[pE])

            def rmrow_ev(eng, h=h):
                return eng.tensor_copy(rowrow[0:1, 256 * h:256 * (h + 1)], pE[0:1, 0:256])
            S.add(f"rmrow_ev_{h}", "vector", rmrow_ev, reads=[pE], writes=[rowrow])

            for qc, pp, bandc in ((0, pA, band0), (1, pB, band1)):
                def nrm(eng, pp=pp, bandc=bandc, qc=qc):
                    return eng.scalar_tensor_tensor(nrmtmp[:, 256 * qc:256 * (qc + 1)], pp[:, 0:256],
                                                    rmxc[:, qc:qc + 1], bandc[:],
                                                    op0=ALU.subtract, op1=ALU.add)
                S.add(f"nrm_{h}_{qc}", "vector", nrm, reads=[pp, rmxc, bandc], writes=[nrmtmp])

            @dma_step(2)
            def nrm_dma(eng, h=h):
                return [eng.dma_start(
                    out=normall[8 * h + 4 * qc: 8 * h + 4 * (qc + 1), :].rearrange("p (a b) -> p a b", a=32),
                    in_=nrmtmp[:, 256 * qc:256 * (qc + 1)]) for qc in (0, 1)]
            S.add(f"nrm_dma_{h}", "sync", nrm_dma, reads=[nrmtmp], writes=[normall])

        for j in range(BISECT_ITERS):
            def bmid_mm(eng):
                return eng.matmul(pE[0:128, 0:1], expind[:], bis[0:16, 2:3], start=True, stop=True)
            S.add(f"bis_bc_{j}", "tensor", bmid_mm, reads=[expind, bis], writes=[pE])

            def bmid_cp(eng):
                return eng.tensor_copy(midcol[:], pE[0:128, 0:1])
            S.add(f"bis_cp_{j}", "vector", bmid_cp, reads=[pE], writes=[midcol])

            def bcmp(eng):
                return eng.tensor_scalar(cmpjunk[:], normall[:], midcol[:], None,
                                         op0=ALU.is_ge, op1=ALU.add, accum_out=acc[:])
            S.add(f"bis_cmp_{j}", "vector", bcmp, reads=[normall, midcol], writes=[cmpjunk, acc])

            def bcnt(eng):
                return eng.matmul(pE[0:16, 4:5], grpind[:], acc[:], start=True, stop=True)
            S.add(f"bis_cnt_{j}", "tensor", bcnt, reads=[grpind, acc], writes=[pE])

            def bge(eng):
                return eng.tensor_scalar(bis[:, 3:4], pE[0:16, 4:5], 1024.0, None, op0=ALU.is_ge)
            S.add(f"bis_ge_{j}", "vector", bge, reads=[pE], writes=[bis])

            def bgeinv(eng):
                return eng.tensor_scalar(bis[:, 5:6], bis[:, 3:4], 0.5, None, op0=ALU.is_lt)
            S.add(f"bis_gi_{j}", "vector", bgeinv, reads=[bis], writes=[bis])

            def bcast0(eng):
                return eng.tensor_copy(bisI[:, 0:1], bis[:, 3:4])
            S.add(f"bis_c0_{j}", "vector", bcast0, reads=[bis], writes=[bisI])

            def bcast1(eng):
                return eng.tensor_copy(bisI[:, 1:2], bis[:, 5:6])
            S.add(f"bis_c1_{j}", "vector", bcast1, reads=[bis], writes=[bisI])

            def blo(eng):
                return eng.copy_predicated(bis[:, 0:1], bisI[:, 0:1], bis[:, 2:3])
            S.add(f"bis_lo_{j}", "vector", blo, reads=[bis, bisI], writes=[bis])

            def bhi(eng):
                return eng.copy_predicated(bis[:, 1:2], bisI[:, 1:2], bis[:, 2:3])
            S.add(f"bis_hi_{j}", "vector", bhi, reads=[bis, bisI], writes=[bis])

            def badd(eng):
                return eng.tensor_tensor(bis[:, 4:5], bis[:, 0:1], bis[:, 1:2], ALU.add)
            S.add(f"bis_add_{j}", "vector", badd, reads=[bis], writes=[bis])

            def bsc(eng):
                return eng.tensor_scalar(bis[:, 2:3], bis[:, 4:5], 0.5, None, op0=ALU.mult)
            S.add(f"bis_sc_{j}", "vector", bsc, reads=[bis], writes=[bis])

        def lorow_mm(eng):
            return eng.matmul(pE[0:1, 5:21], bis[0:16, 0:1], identf[0:16, 0:16], start=True, stop=True)
        S.add("lorow_mm", "tensor", lorow_mm, reads=[bis, identf], writes=[pE])

        def lorow_cp(eng):
            return eng.tensor_copy(lorow[:], pE[0:1, 5:21])
        S.add("lorow_cp", "vector", lorow_cp, reads=[pE], writes=[lorow])

        def tha_mm(eng):
            return eng.matmul(pE[0:128, 32:48], onesfr[:], lorow[:], start=True, stop=True)
        S.add("tha_mm", "tensor", tha_mm, reads=[onesfr, lorow], writes=[pE])

        def tha_cp(eng):
            return eng.tensor_copy(threshall[:], pE[0:128, 32:48])
        S.add("tha_cp", "vector", tha_cp, reads=[pE], writes=[threshall])

        self._p2 = dict(locals())

    def emit_p2_heads(self, heads, interleave=None):
        S = self.sched
        L = self._p2
        (pA, pB, pC, pD, pT1, pT2, pE) = (self.psum[k] for k in ("A", "B", "C", "D", "T1", "T2", "E"))
        ident = self.ident
        QT, KH, VH, ATTNO, VSTG = self.QT, self.KH, self.VH, self.ATTNO, self.VSTG
        (threshall, onesfr, rowrow, lowTall, band0, band1, onesbb, siota1, ltinc,
         siota128, kbiota, onesf, Vmt, onesrow2, pattc, neg1e9row, lnepscol, normall) = (
            L[k] for k in ("threshall", "onesfr", "rowrow", "lowTall", "band0", "band1",
                           "onesbb", "siota1", "ltinc", "siota128", "kbiota",
                           "onesf", "Vmt", "onesrow2", "pattc", "neg1e9row", "lnepscol",
                           "normall"))
        self.preset_to("BF", self._bf_mark)
        self.preset_to("F32", self._f32_mark)
        self.preset("W")
        qthead = self.palloc("W", [64, T], "qthead")
        Kstg = [self.palloc("BF", [128, 2048], f"Kstg{j}") for j in range(2)]
        Vstg = [self.palloc("BF", [128, 2048], f"Vstg{j}") for j in range(2)]
        KTall = [self.palloc("BF", [64, 128, 32], f"KTall{j}") for j in range(2)]
        onehot = self.palloc("BF", [128, 2, 1024], "onehot")
        exps = [self.palloc("BF", [128, 512], f"exps{j}") for j in range(2)]
        Vgt = [self.palloc("BF", [128, 4, 66], f"Vgt{j}") for j in range(2)] + \
              [self.palloc("W", [128, 4, 66], f"Vgt{j}") for j in range(2, 4)]
        invrowb = self.palloc("BF", [1, 1024], "invrowb")
        nq = self.palloc("F32", [128, 512], "nq")
        maskQ = self.sb("maskQ", [128, 2, 256], BF16)
        maskT = self.sb("maskTb", [128, 2, 256], BF16)
        spexcl = self.palloc("F32", [128, 2, 64], "spexcl")
        ohtmp = self.palloc("F32", [128, 1024], "ohtmp")
        explowT = self.palloc("F32", [128, 512], "explowT")
        w32 = self.palloc("F32", [128, 512], "w32")
        kbidxr = self.palloc("F32", [1, 1024], "kbidxr")
        invrow = self.palloc("F32", [1, 1024], "invrow")
        umT = self.sb("umT", [128, 2, 64], BF16)
        negmaskT = self.sb("negmaskT", [128, 2, 256], BF16)
        lowattnT = self.sb("lowattnT", [128, 2, 256], BF16)
        lowoutsb = self.sb("lowoutsb", [128, 2, 66], BF16)
        patt4x = self.palloc("W", [16, 128], "patt4x")
        cmask16 = self.palloc("W", [16, 512], "cmask16")
        mexp = [self.palloc("W", [16, 512], f"mexp{j}") for j in range(2)]
        idxint = self.sb("idxint", [128, 8], I32)
        attbig = [self.palloc("W", [128, 8, 64], f"attbig{j}") for j in range(2)]
        stgn = [self.palloc("W", [128, 8, 64], f"stgn{j}") for j in range(2)]
        stgd = [self.palloc("F32", [128, 8, 2], f"stgd{j}") for j in range(2)]
        cmbs = [self.palloc("F32", [128, 40], f"cmbs{j}") for j in range(2)]
        cmbb = [self.palloc("BF", [128, 8], f"cmbb{j}") for j in range(2)]

        @dma_step(2)
        def pq_ld(eng):
            return [eng.dma_start(out=patt4x[:], in_=self.cst["patt4x"]),
                    eng.dma_start(out=cmask16[:], in_=self.cst["cmask16"])]
        S.add("pq_ld", "sync", pq_ld, writes=[patt4x, cmask16])

        def vgt_z2(eng):
            for j in range(4):
                ii = eng.memset(Vgt[j][:], 0.0)
            return ii
        S.add("vgt_z2", "vector", vgt_z2, writes=Vgt)

        def vgt_ones2(eng):
            for j in range(4):
                ii = eng.memset(Vgt[j][:, :, 64:65], 1.0)
            return ii
        S.add("vgt_ones2", "vector", vgt_ones2, reads=Vgt, writes=Vgt)

        for h in heads:
            if interleave is not None:
                interleave(h)
            hp = h % 2

            @dma_step(1)
            def vm_ld(eng, h=h):
                return eng.dma_start(out=Vmt[:].rearrange("p a b -> p (a b)"), in_=self.VMH[h])
            S.add(f"vm_ld_{h}", "sync", vm_ld, reads=[self.VMH], writes=[Vmt])

            @dma_step(2)
            def nq_ld(eng, h=h):
                return [eng.dma_start(
                    out=nq[:, 256 * qc:256 * (qc + 1)],
                    in_=normall[8 * h + 4 * qc: 8 * h + 4 * (qc + 1), :].rearrange("p (a b) -> p a b", a=32))
                    for qc in (0, 1)]
            S.add(f"nq_ld_{h}", "sync", nq_ld, reads=[normall], writes=[nq])

            def mkQ(eng, h=h):
                i1 = None
                for qc in range(2):
                    i1 = eng.tensor_scalar(maskQ[:, qc, :], nq[:, 256 * qc:256 * (qc + 1)],
                                           threshall[:, h:h + 1], None, op0=ALU.is_ge)
                return i1
            S.add(f"mkQ_{h}", "vector", mkQ, reads=[nq, threshall], writes=[maskQ])

            def mkT_tr(eng, h=h):
                i1 = None
                for qc in range(2):
                    for r in range(2):
                        i1 = eng.transpose(pT1[:, 128 * (2 * qc + r):128 * (2 * qc + r + 1)],
                                           maskQ[:, qc, 128 * r:128 * (r + 1)], ident[:])
                return i1
            S.add(f"mkT_tr_{h}", "tensor", mkT_tr, reads=[maskQ, ident], writes=[pT1])

            def mkT_ev(eng, h=h):
                i1 = None
                for qc in range(2):
                    for r in range(2):
                        i1 = eng.activation(maskT[:, r, 128 * qc:128 * (qc + 1)],
                                            pT1[:, 128 * (2 * qc + r):128 * (2 * qc + r + 1)],
                                            AF.Copy, bias=0.0, scale=1.0)
                return i1
            S.add(f"mkT_ev_{h}", "scalar", mkT_ev, reads=[pT1], writes=[maskT])

            def negm(eng, h=h):
                i1 = None
                for r in range(2):
                    i1 = eng.tensor_scalar(negmaskT[:, r, :], maskT[:, r, :], 1.0, 1e9,
                                           op0=ALU.subtract, op1=ALU.mult)
                return i1
            S.add(f"negm_{h}", "vector", negm, reads=[maskT], writes=[negmaskT])

            def elow(eng, h=h):
                return eng.activation(explowT[:], lowTall[:, h, :], AF.Exp, bias=0.0, scale=1.0)
            S.add(f"elow_{h}", "scalar", elow, reads=[lowTall], writes=[explowT])

            def w32mk(eng, h=h):
                return eng.tensor_scalar(w32[:], maskT[:].rearrange("p a b -> p (a b)"), -32.0, 32.0,
                                         op0=ALU.mult, op1=ALU.add)
            S.add(f"w32_{h}", "vector", w32mk, reads=[maskT], writes=[w32])

            def lattn(eng, h=h):
                return eng.tensor_tensor(lowattnT[:].rearrange("p a b -> p (a b)"), explowT[:], w32[:],
                                         ALU.mult)
            S.add(f"lattn_{h}", "vector", lattn, reads=[explowT, w32], writes=[lowattnT])

            def umT_mk(eng, h=h):
                i1 = None
                for r in range(2):
                    i1 = eng.tensor_reduce(umT[:, r, :],
                                           maskT[:, r, :].rearrange("p (g q) -> p g q", g=64),
                                           axis=AX.X, op=ALU.max)
                return i1
            S.add(f"umT_{h}", "vector", umT_mk, reads=[maskT], writes=[umT])

            def cnts_mm(eng, h=h):
                i1 = None
                for r in range(2):
                    i1 = eng.matmul(pE[0:1, 256:320], onesbb[:, 0:1], umT[:, r, :],
                                    start=(r == 0), stop=(r == 1))
                return i1
            S.add(f"cnts_{h}", "tensor", cnts_mm, reads=[onesbb, umT], writes=[pE])

            def inv_mk(eng, h=h):
                return eng.tensor_tensor(invrow[:], siota1[:],
                                         pE[0:1, 256:320].to_broadcast([1, 64, 16]), ALU.is_ge)
            S.add(f"inv_{h}", "vector", inv_mk, reads=[siota1, pE], writes=[invrow])

            def invb(eng, h=h):
                return eng.tensor_copy(invrowb[:], invrow[:])
            S.add(f"invb_{h}", "vector", invb, reads=[invrow], writes=[invrowb])

            def spos_mm(eng, h=h):
                eng.matmul(pE[:, 320:384], ltinc[:], umT[:, 0, :], start=True, stop=True)
                eng.matmul(pE[:, 384:448], onesbb[:], umT[:, 0, :], start=True, stop=False)
                return eng.matmul(pE[:, 384:448], ltinc[:], umT[:, 1, :], start=False, stop=True)
            S.add(f"spos_{h}", "tensor", spos_mm, reads=[ltinc, onesbb, umT], writes=[pE])

            def spex(eng, h=h):
                i1 = None
                for r in range(2):
                    i1 = eng.tensor_tensor(spexcl[:, r, :], pE[:, 320 + 64 * r:384 + 64 * r],
                                           umT[:, r, :], ALU.subtract)
                return i1
            S.add(f"spex_{h}", "vector", spex, reads=[pE, umT], writes=[spexcl])

            for r in range(2):
                def oh1(eng, h=h, r=r):
                    return eng.tensor_tensor(ohtmp[:], spexcl[:, r, :].to_broadcast([128, 64, 16]),
                                             siota128[:], ALU.is_equal)
                S.add(f"oh1_{h}_{r}", "vector", oh1, reads=[spexcl, siota128], writes=[ohtmp])

                def oh2(eng, h=h, r=r):
                    return eng.tensor_tensor(ohtmp[:], ohtmp[:],
                                             umT[:, r, :].to_broadcast([128, 64, 16]), ALU.mult)
                S.add(f"oh2_{h}_{r}", "vector", oh2, reads=[ohtmp, umT], writes=[ohtmp])

                def oh3(eng, h=h, r=r):
                    return eng.tensor_copy(onehot[:, r, :], ohtmp[:])
                S.add(f"oh3_{h}_{r}", "vector", oh3, reads=[ohtmp], writes=[onehot])

            def kbx_mm(eng, h=h):
                i1 = None
                for nc2, pp in ((0, pA), (1, pB)):
                    for r in range(2):
                        i1 = eng.matmul(pp[0:1, 0:512], kbiota[:, r:r + 1],
                                        onehot[:, r, 512 * nc2:512 * (nc2 + 1)],
                                        start=(r == 0), stop=(r == 1))
                return i1
            S.add(f"kbx_{h}", "tensor", kbx_mm, reads=[kbiota, onehot], writes=[pA, pB])

            def kbx_ev(eng, h=h):
                eng.tensor_copy(kbidxr[0:1, 0:512], pA[0:1, 0:512])
                return eng.tensor_copy(kbidxr[0:1, 512:1024], pB[0:1, 0:512])
            S.add(f"kbxe_{h}", "vector", kbx_ev, reads=[pA, pB], writes=[kbidxr])

            def idx_mm(eng, h=h):
                i1 = None
                for j in range(8):
                    i1 = eng.matmul(pC[0:128, j:j + 1], kbidxr[0:1, 128 * j:128 * (j + 1)],
                                    onesf[0:1, 0:1], start=True, stop=True)
                return i1
            S.add(f"idxm_{h}", "tensor", idx_mm, reads=[kbidxr, onesf], writes=[pC])

            def idx_ev(eng, h=h):
                return eng.tensor_copy(idxint[:], pC[0:128, 0:8])
            S.add(f"idxe_{h}", "vector", idx_ev, reads=[pC], writes=[idxint])

            def lowo_mm(eng, h=h):
                last = None
                for qc in range(2):
                    for r in range(2):
                        eng.matmul(pD[:, 67 * qc:67 * qc + 65], lowattnT[:, r, 128 * qc:128 * (qc + 1)],
                                   Vmt[:, r, 0:65], start=(r == 0), stop=(r == 1))
                    last = eng.matmul(pD[:, 67 * qc + 65:67 * qc + 66],
                                      rowrow[0:1, 256 * h + 128 * qc:256 * h + 128 * (qc + 1)],
                                      onesrow2[0:1, 0:1], start=True, stop=True)
                return last
            S.add(f"lowo_{h}", "tensor", lowo_mm, reads=[lowattnT, Vmt, rowrow, onesrow2], writes=[pD])

            def lowo_ev(eng, h=h):
                i1 = None
                for qc in range(2):
                    i1 = eng.activation(lowoutsb[:, qc, 0:66], pD[:, 67 * qc:67 * qc + 66],
                                        AF.Copy, bias=0.0, scale=1.0)
                return i1
            S.add(f"lowoe_{h}", "scalar", lowo_ev, reads=[pD], writes=[lowoutsb])

            @dma_step(1)
            def qt_ld(eng, h=h):
                return eng.dma_start(out=qthead[:], in_=QT[64 * h:64 * (h + 1), :])
            S.add(f"qtld_{h}", "sync", qt_ld, reads=[QT], writes=[qthead])

            for Ti in range(8):
                tb = Ti % 2

                @dma_step(1, grouped=True)
                def kgat(eng, h=h, Ti=Ti, tb=tb):
                    return eng.indirect_dma_start(
                        out=Kstg[tb][:], out_offset=None,
                        in_=KH.rearrange("h (n k) d -> (h n) (k d)", k=32),
                        in_offset=IndirectOffsetOnAxis(idxint[:, Ti:Ti + 1], 0),
                        element_offset=h * 256 * 2048)
                S.add(f"kg_{h}_{Ti}", "gpsimd", kgat, reads=[KH, idxint], writes=[Kstg[tb]])

                @dma_step(1, grouped=True)
                def vgat(eng, h=h, Ti=Ti, tb=tb):
                    return eng.indirect_dma_start(
                        out=Vstg[tb][:], out_offset=None,
                        in_=VH.rearrange("h (n k) d -> (h n) (k d)", k=32),
                        in_offset=IndirectOffsetOnAxis(idxint[:, Ti:Ti + 1], 0),
                        element_offset=h * 256 * 2048)
                S.add(f"vg_{h}_{Ti}", "gpsimd", vgat, reads=[VH, idxint], writes=[Vstg[tb]])

                @dma_step(1)
                def vbnc(eng, h=h, Ti=Ti, tb=tb, hp=hp):
                    return eng.dma_start(out=VSTG[hp, 128 * Ti:128 * (Ti + 1), :], in_=Vstg[tb][:])
                S.add(f"vb_{h}_{Ti}", "gpsimd", vbnc, reads=[Vstg[tb]], writes=[VSTG])

                for a in range(8):
                    ptr = (pT1, pT2)[a % 2]

                    def ktr(eng, h=h, Ti=Ti, tb=tb, a=a, ptr=ptr):
                        i1 = None
                        for b4 in range(4):
                            t = 4 * a + b4
                            i1 = eng.transpose(ptr[0:64, 128 * b4:128 * (b4 + 1)],
                                               Kstg[tb][:, 64 * t:64 * (t + 1)], ident[:])
                        return i1
                    S.add(f"ktr_{h}_{Ti}_{a}", "tensor", ktr, reads=[Kstg[tb], ident], writes=[ptr])

                    def ktev(eng, h=h, Ti=Ti, tb=tb, a=a, ptr=ptr):
                        return eng.activation(KTall[tb][0:64, :, 4 * a:4 * (a + 1)],
                                              ptr[0:64, 0:512].rearrange("p (t s) -> p s t", t=4),
                                              AF.Copy, bias=0.0, scale=1.0)
                    S.add(f"ktev_{h}_{Ti}_{a}", "scalar", ktev, reads=[ptr], writes=[KTall[tb]])

                for gl in range(8):
                    g = 8 * Ti + gl
                    gp = g % 2
                    psc = (pA, pC)[gp]
                    pav = (pB, pD)[gp]

                    g4 = g % 4

                    @dma_step(1, grouped=True)
                    def vgld(eng, h=h, g=g, g4=g4, hp=hp):
                        vv = VSTG[hp].rearrange("r c -> (r c)")[32768 * g:32768 * (g + 1)]
                        return eng.dma_start(
                            out=Vgt[g4][:, :, 0:64],
                            in_=vv.rearrange("(c p e) -> p c e", c=4, p=128))
                    S.add(f"vl_{h}_{g}", ("sync", "scalar")[g % 2], vgld,
                          reads=[VSTG], writes=[Vgt[g4]])

                    mw = 120 + 4 * gp

                    def mskmm(eng, h=h, g=g, mw=mw):
                        for r in range(2):
                            eng.matmul(pE[0:16, mw:mw + 4], onehot[:, r, 16 * g:16 * (g + 1)],
                                       negmaskT[:, r, 4 * g:4 * (g + 1)],
                                       start=(r == 0), stop=False)
                        return eng.matmul(pE[0:16, mw:mw + 4], invrowb[0:1, 16 * g:16 * (g + 1)],
                                          neg1e9row[0:1, 0:4], start=False, stop=True)
                    S.add(f"msk_{h}_{g}", "tensor", mskmm,
                          reads=[onehot, negmaskT, invrowb, neg1e9row], writes=[pE])

                    def mxp(eng, gp=gp, mw=mw):
                        return eng.tensor_tensor(
                            mexp[gp][:].rearrange("p (c qb t) -> p qb c t", c=4, qb=4),
                            cmask16[:].rearrange("p (c qb t) -> p qb c t", c=4, qb=4),
                            pE[0:16, mw:mw + 4].to_broadcast([16, 4, 4, 32]), ALU.mult)
                    S.add(f"mxp_{h}_{g}", "vector", mxp, reads=[pE, cmask16], writes=[mexp[gp]])

                    def scmm(eng, h=h, g=g, gl=gl, tb=tb, psc=psc, gp=gp):
                        eng.matmul(psc[:], patt4x[:], mexp[gp][:], start=True, stop=False)
                        i1 = None
                        for c in range(4):
                            reg = psc[:, 128 * c:128 * (c + 1)]
                            i1 = eng.matmul(reg, KTall[tb][0:64, 16 * gl + 4 * c:16 * gl + 4 * (c + 1), :]
                                            .rearrange("p s t -> p (s t)"),
                                            qthead[0:64, 128 * g:128 * (g + 1)],
                                            start=False, stop=(c == 3))
                        return i1
                    S.add(f"sc_{h}_{g}", "tensor", scmm,
                          reads=[KTall[tb], qthead, patt4x, mexp[gp]],
                          writes=[psc])

                    def expmk(eng, h=h, g=g, gp=gp, psc=psc):
                        return eng.activation(exps[gp][:], psc[:], AF.Exp, bias=0.0, scale=1.0)
                    S.add(f"ex_{h}_{g}", "scalar", expmk, reads=[psc], writes=[exps[gp]])

                    def avmm(eng, h=h, g=g, gp=gp, g4=g4, pav=pav):
                        eng.matmul(pav[:, 0:66], pattc[:, 128 * (g % 32):128 * (g % 32) + 128],
                                   lowoutsb[:, g // 32, 0:66], start=True, stop=False)
                        i1 = None
                        for c in range(4):
                            i1 = eng.matmul(pav[:, 0:65], exps[gp][:, 128 * c:128 * (c + 1)],
                                            Vgt[g4][:, c, 0:65], start=False, stop=(c == 3))
                        return i1
                    S.add(f"av_{h}_{g}", "tensor", avmm,
                          reads=[exps[gp], Vgt[g4], pattc, lowoutsb], writes=[pav])

                    bb = (g // 8) % 2

                    def cp1(eng, g=g, pav=pav, bb=bb):
                        return eng.activation(stgn[bb][:, g % 8, :], pav[:, 0:64],
                                              AF.Copy, bias=0.0, scale=1.0)
                    S.add(f"cp1_{h}_{g}", "scalar", cp1, reads=[pav], writes=[stgn[bb]])

                    def cp2(eng, g=g, pav=pav, bb=bb):
                        return eng.tensor_copy(stgd[bb][:, g % 8, :], pav[:, 64:66])
                    S.add(f"cp2_{h}_{g}", "vector", cp2, reads=[pav], writes=[stgd[bb]],
                          deps=[f"cp1_{h}_{g}"])

                    if g % 8 == 7:
                        d0 = stgd[bb][:, :, 0:1].rearrange("p a b -> p (a b)")
                        d1 = stgd[bb][:, :, 1:2].rearrange("p a b -> p (a b)")

                        def b1(eng, bb=bb, d0=d0):
                            return eng.activation(cmbs[bb][:, 0:8], d0,
                                                  AF.Ln, bias=0.0, scale=1.0)
                        S.add(f"bc1_{h}_{g}", "scalar", b1, reads=[stgd[bb]], writes=[cmbs[bb]])

                        def b2(eng, bb=bb, d1=d1):
                            return eng.tensor_tensor(cmbs[bb][:, 8:16], cmbs[bb][:, 0:8],
                                                     d1, ALU.max)
                        S.add(f"bc2_{h}_{g}", "vector", b2, reads=[cmbs[bb], stgd[bb]],
                              writes=[cmbs[bb]])

                        def b3(eng, bb=bb):
                            return eng.activation(cmbs[bb][:, 16:24], cmbs[bb][:, 8:16], AF.Exp,
                                                  bias=lnepscol[:], scale=1.0)
                        S.add(f"bc3_{h}_{g}", "scalar", b3, reads=[cmbs[bb], lnepscol],
                              writes=[cmbs[bb]])

                        def b4(eng, bb=bb, d0=d0):
                            return eng.tensor_tensor(cmbs[bb][:, 24:32], d0,
                                                     cmbs[bb][:, 16:24], ALU.add)
                        S.add(f"bc4_{h}_{g}", "vector", b4, reads=[cmbs[bb], stgd[bb]],
                              writes=[cmbs[bb]])

                        def b5(eng, bb=bb):
                            return eng.reciprocal(cmbs[bb][:, 32:40], cmbs[bb][:, 24:32])
                        S.add(f"bc5_{h}_{g}", "vector", b5, reads=[cmbs[bb]], writes=[cmbs[bb]])

                        def b5c(eng, bb=bb):
                            return eng.tensor_copy(cmbb[bb][:], cmbs[bb][:, 32:40])
                        S.add(f"bc5c_{h}_{g}", "vector", b5c, reads=[cmbs[bb]], writes=[cmbb[bb]])

                        def b6(eng, bb=bb):
                            return eng.tensor_tensor(attbig[bb][:], stgn[bb][:],
                                                     cmbb[bb][:].to_broadcast([128, 8, 64]),
                                                     ALU.mult)
                        S.add(f"bc6_{h}_{g}", "vector", b6,
                              reads=[stgn[bb], cmbb[bb]], writes=[attbig[bb]])

                        @dma_step(1, grouped=True)
                        def ast(eng, h=h, g=g, bb=bb):
                            gb = g // 8
                            return eng.dma_start(
                                out=ATTNO[h, 1024 * gb:1024 * (gb + 1), :]
                                    .rearrange("(s p) d -> p s d", p=128),
                                in_=attbig[bb][:])
                        S.add(f"as_{h}_{g}", "scalar", ast, reads=[attbig[bb]], writes=[ATTNO])

    def _ln_steps(self, S, tag, i, st3, junk3, sap, stok, dst, extra_in, o):
        epscol = self.epscol

        def v_red(eng, sap=sap, o=o):
            return eng.tensor_reduce(st3[:, o:o + 1], sap, axis=AX.X, op=ALU.add)
        S.add(f"{tag}_red_{i}", "vector", v_red, reads=[stok], writes=[st3])

        def s_sq(eng, sap=sap, o=o):
            return eng.activation(junk3[:], sap, AF.Square, bias=0.0, scale=1.0,
                                  accum_out=st3[:, o + 1:o + 2])
        S.add(f"{tag}_sq_{i}", "scalar", s_sq, reads=[stok], writes=[st3, junk3])

        def v_mean(eng, o=o):
            return eng.tensor_scalar(st3[:, o + 2:o + 3], st3[:, o:o + 1], 1.0 / D, None, op0=ALU.mult)
        S.add(f"{tag}_mean_{i}", "vector", v_mean, reads=[st3], writes=[st3])

        def v_msq(eng, o=o):
            return eng.tensor_tensor(st3[:, o + 3:o + 4], st3[:, o + 2:o + 3], st3[:, o + 2:o + 3], ALU.mult)
        S.add(f"{tag}_msq_{i}", "vector", v_msq, reads=[st3], writes=[st3])

        def v_var(eng, o=o):
            return eng.tensor_scalar(st3[:, o + 4:o + 5], st3[:, o + 1:o + 2], 1.0 / D,
                                     st3[:, o + 3:o + 4], op0=ALU.mult, op1=ALU.subtract)
        S.add(f"{tag}_var_{i}", "vector", v_var, reads=[st3], writes=[st3])

        def s_sqrt(eng, o=o):
            return eng.activation(st3[:, o + 5:o + 6], st3[:, o + 4:o + 5], AF.Sqrt,
                                  bias=epscol[:], scale=1.0)
        S.add(f"{tag}_sqrt_{i}", "scalar", s_sqrt, reads=[st3, epscol], writes=[st3])

        def v_recip(eng, o=o):
            return eng.reciprocal(st3[:, o + 6:o + 7], st3[:, o + 5:o + 6])
        S.add(f"{tag}_recip_{i}", "vector", v_recip, reads=[st3], writes=[st3])

        def v_mb(eng, o=o):
            return eng.tensor_scalar(st3[:, o + 7:o + 8], st3[:, o + 2:o + 3],
                                     st3[:, o + 6:o + 7], -1.0, op0=ALU.mult, op1=ALU.mult)
        S.add(f"{tag}_mb_{i}", "vector", v_mb, reads=[st3], writes=[st3])

        if extra_in is not None:
            def v_out1(eng, sap=sap, dst=dst, o=o):
                return eng.tensor_scalar(dst[:], sap, st3[:, o + 6:o + 7],
                                         st3[:, o + 7:o + 8], op0=ALU.mult, op1=ALU.add)
            S.add(f"{tag}_out1_{i}", "vector", v_out1, reads=[stok, st3], writes=[dst])

            def v_out2(eng, dst=dst, ei=extra_in):
                return eng.tensor_tensor(dst[:], dst[:], ei[:], ALU.add)
            S.add(f"{tag}_out_{i}", "vector", v_out2, reads=[dst, extra_in], writes=[dst])
        else:
            def v_out(eng, sap=sap, dst=dst, o=o):
                return eng.tensor_scalar(dst[:], sap, st3[:, o + 2:o + 3],
                                         st3[:, o + 6:o + 7], op0=ALU.subtract, op1=ALU.mult)
            S.add(f"{tag}_out_{i}", "vector", v_out, reads=[stok, st3], writes=[dst])

    def declare_p36(self):
        self.X1 = self.dint("X1", [T, D], F32)
        self.XF = self.dint("XF", [T, D], BF16)
        self.XFT = self.dint("XFT", [D, T], BF16)
        self.LISTS = self.dint("LISTS", [NE * 8192, 2], F32)
        self.YS = self.dint("YS", [NT * 1024, D], BF16)
        self.WB1R = self.dint("WB1R", [NE * 4 * 128, 8192], BF16)
        self.WB2R = self.dint("WB2R", [NE * 4 * 128, 8192], BF16)
        self.SH = self.dint("SH", [T, D], F32)
        self.OUT = self.dout("OUT", [T, D], F32)
        self.rW = self.din("rW", [D, NE], F32)
        self.rb = self.din("rb", [NE], F32)
        self.W1 = self.din("W1", [NE, D, HID], F32)
        self.b1 = self.din("b1", [NE, HID], F32)
        self.W2 = self.din("W2", [NE, HID, D], F32)
        self.b2 = self.din("b2", [NE, D], F32)
        self.sW1 = self.din("sW1", [NSH, D, HID], F32)
        self.sb1 = self.din("sb1", [NSH, HID], F32)
        self.sW2 = self.din("sW2", [NSH, HID, D], F32)
        self.sb2 = self.din("sb2", [NSH, D], F32)
        self.SIDX = self.sb("SIDX", [128, 64, 2], I32)
        self.SIDXE = self.sb("SIDXE", [128, 64, 2], F32)
        self.SIDXS = self.sb("SIDXS", [128, 64, 2], F32)

    def emit_p3(self):
        S = self.sched
        self.preset("BF"); self.preset("F32"); self.preset("W")
        ident, epscol = self.ident, self.epscol
        Xt3 = [self.palloc("F32", [128, 1024], f"Xt3{i}") for i in range(2)]
        att = [self.palloc("BF", [128, 16, 64], f"att{i}") for i in range(2)]
        tmp3 = [self.palloc("F32", [128, 1024], f"tmp3{i}") for i in range(2)]
        x1t = [self.palloc("F32", [128, 1024], f"x1t{i}") for i in range(2)]
        xft = [self.palloc("BF", [128, 1024], f"xft{i}") for i in range(2)]
        st3 = [self.sb(f"st3{i}", [128, 16], F32) for i in range(2)]
        self._p3mark_bf = self.pmark("BF")
        self._p3mark_f32 = self.pmark("F32")
        X, ATTNO, X1, XF = self.X, self.ATTNO, self.X1, self.XF

        junk3 = [self.palloc("BF", [128, 1024], f"junk3{i}") for i in range(2)]
        self._p3mark_bf = self.pmark("BF")

        for i in range(64):
            b = i % 2

            @dma_step(1, grouped=True)
            def x_ld(eng, i=i, b=b):
                return eng.dma_start(out=Xt3[b][:], in_=X[128 * i:128 * (i + 1), :])
            S.add(f"p3x_{i}", "sync", x_ld, reads=[X], writes=[Xt3[b]])

            @dma_step(16, grouped=True)
            def a_ld(eng, i=i, b=b):
                return [eng.dma_start(out=att[b][:, h, :], in_=ATTNO[h, 128 * i:128 * (i + 1), :])
                        for h in range(16)]
            S.add(f"p3a_{i}", "gpsimd", a_ld, reads=[ATTNO], writes=[att[b]])

            attv = att[b][:].rearrange("p a b -> p (a b)")
            self._ln_steps(S, "p3ln1", i, st3[b], junk3[b], attv, att[b], x1t[b], Xt3[b], 0)
            self._ln_steps(S, "p3ln2", i, st3[b], junk3[b], x1t[b][:], x1t[b], xft[b], None, 8)

            @dma_step(1, grouped=True)
            def st_x1(eng, i=i, b=b):
                return eng.dma_start(out=X1[128 * i:128 * (i + 1), :], in_=x1t[b][:])
            S.add(f"p3st1_{i}", "sync", st_x1, reads=[x1t[b]], writes=[X1])

            @dma_step(1, grouped=True)
            def st_xf(eng, i=i, b=b):
                return eng.dma_start(out=XF[128 * i:128 * (i + 1), :], in_=xft[b][:])
            S.add(f"p3st_{i}", "scalar", st_xf, reads=[xft[b]], writes=[XF])

    def emit_p4a(self):
        S = self.sched
        self.preset_to("BF", self._p3mark_bf)
        self.preset_to("F32", self._p3mark_f32)
        ident = self.ident
        XF, XFT, LISTS = self.XF, self.XFT, self.LISTS
        pA, pB, pC, pD, pT1, pT2, pE = (self.psum[k] for k in ("A", "B", "C", "D", "T1", "T2", "E"))
        xf4 = [self.palloc("BF", [128, 4, 1024], f"xf4{i}") for i in range(2)]
        xfT4 = [self.palloc("BF", [128, 8, 512], f"xfT4{i}") for i in range(2)]
        rwt = self.sb("rwt", [128, 8, 8], BF16)
        rbrow = self.sb("rbrow", [1, 8], BF16)
        gate4 = [self.sb(f"gate4{i}", [128, 4, 8], F32) for i in range(2)]
        r4 = [self.sb(f"r4{i}", [128, 32], F32) for i in range(2)]
        ltexf = self.sb("ltexf", [128, 128], F32)
        eio = self.sb("eio", [128, 8], F32)
        base8 = self.sb("base8", [1, 16], F32)
        payl = [self.sb(f"payl{i}", [128, 2, 2], F32) for i in range(2)]
        sct = [self.sb(f"sct{i}", [128, 4], I32) for i in range(2)]
        self._lp = [object() for _ in range(8)]
        tokio = self.sb("tokio", [128, 1], F32)
        onesb512 = self.onesrow
        gevt = self.sb("gevt", [8, 512], BF16)
        identf8 = self.sb("identf8", [8, 8], F32)
        junk8 = [self.sb(f"junk8{i}", [128, 8], F32) for i in range(2)]
        me8 = [self.sb(f"me8{i}", [128, 8], F32) for i in range(2)]
        slot8 = [self.sb(f"slot8{i}", [128, 8], F32) for i in range(2)]
        onescolf = self.sb("onescolf", [128, 4], F32)
        onesfr = self.onesfr
        SIDX = self.SIDX

        @dma_step(9)
        def r_load(eng):
            ins = [eng.dma_start(out=rwt[:, kc, :], in_=self.rW[128 * kc:128 * (kc + 1), :])
                   for kc in range(8)]
            ins.append(eng.dma_start(out=rbrow[:], in_=self.rb.rearrange("(b a) -> b a", b=1)))
            return ins
        S.add("r_load", "gpsimd", r_load, reads=[self.rW, self.rb], writes=[rwt, rbrow])

        @dma_step(1)
        def lt_load(eng):
            return eng.dma_start(out=ltexf[:], in_=self.cst["ltexclf"])
        S.add("lt_load", "sync", lt_load, writes=[ltexf])

        @dma_step(4)
        def r_load2(eng):
            return [eng.dma_start(out=eio[:], in_=self.cst["eiota"]),
                    eng.dma_start(out=tokio[:], in_=self.cst["tokiota"]),
                    eng.dma_start(out=identf8[:], in_=self.cst["identf"][0:8, 0:8]),
                    eng.dma_start(out=onescolf[:], in_=self.cst["ones_f32"])]
        S.add("r_load2", "sync", r_load2, writes=[eio, tokio, identf8, onescolf])



        @dma_step(1)
        def z_st(eng):
            return eng.dma_start(out=LISTS.rearrange("a b -> (a b)").rearrange("(p f) -> p f", p=128),
                                 in_=self.cst["zeros128"])
        S.add("z_st", "gpsimd", z_st, writes=[LISTS])

        def b_init(eng):
            return eng.memset(base8[:], 0.0)
        S.add("b_init", "vector", b_init, writes=[base8])
        self.eio, self.tokio, self.junk8, self.ltexf = eio, tokio, junk8[0], ltexf
        self.identf8, self.base8, self.onescolf = identf8, base8, onescolf

        for i in range(16):
            b = i % 2

            @dma_step(1, grouped=True)
            def xf_ld(eng, i=i, b=b):
                return eng.dma_start(out=xf4[b][:],
                                     in_=XF[512 * i:512 * (i + 1), :].rearrange("(s p) d -> p s d", p=128))
            S.add(f"r_xfld_{i}", "sync", xf_ld, reads=[XF], writes=[xf4[b]])

            for s in range(4):
                for g4 in range(2):
                    ptr = (pT1, pT2)[g4]

                    def xtr(eng, b=b, s=s, g4=g4, ptr=ptr):
                        i1 = None
                        for a in range(4):
                            fc = 4 * g4 + a
                            i1 = eng.transpose(ptr[:, 128 * a:128 * (a + 1)],
                                               xf4[b][:, s, 128 * fc:128 * (fc + 1)], ident[:])
                        return i1
                    S.add(f"r_xtr_{i}_{s}_{g4}", "tensor", xtr, reads=[xf4[b], ident], writes=[ptr])

                    def xev(eng, b=b, s=s, g4=g4, ptr=ptr):
                        return eng.activation(xfT4[b][:, 4 * g4:4 * (g4 + 1), 128 * s:128 * (s + 1)],
                                              ptr[:].rearrange("p (a q) -> p a q", a=4),
                                              AF.Copy, bias=0.0, scale=1.0)
                    S.add(f"r_xev_{i}_{s}_{g4}", "scalar", xev, reads=[ptr], writes=[xfT4[b]])

            @dma_step(8, grouped=True)
            def xft_st(eng, i=i, b=b):
                return [eng.dma_start(out=XFT[128 * fc:128 * (fc + 1), 512 * i:512 * (i + 1)],
                                      in_=xfT4[b][:, fc, :]) for fc in range(8)]
            S.add(f"r_xftst_{i}", "scalar", xft_st, reads=[xfT4[b]], writes=[XFT])

            def g_mm(eng, b=b):
                for kc in range(8):
                    eng.matmul(pE[0:8, 0:512], rwt[:, kc, :], xfT4[b][:, kc, :].rearrange("p a -> p a"),
                               start=(kc == 0), stop=False)
                return eng.matmul(pE[0:8, 0:512], rbrow[:], onesb512[:], start=False, stop=True)
            S.add(f"g_mm_{i}", "tensor", g_mm, reads=[rwt, xfT4[b], rbrow, onesb512], writes=[pE])

            def g_ev(eng, i=i):
                return eng.tensor_copy(gevt[0:8, :], pE[0:8, 0:512])
            S.add(f"g_ev_{i}", "vector", g_ev, reads=[pE], writes=[gevt])

            def g_tr2(eng, i=i):
                i1 = None
                for s in range(4):
                    i1 = eng.transpose(pT1[0:128, 8 * s:8 * (s + 1)],
                                       gevt[0:8, 128 * s:128 * (s + 1)], self.ident[0:8, 0:8])
                return i1
            S.add(f"g_tr_{i}", "tensor", g_tr2, reads=[gevt, self.ident], writes=[pT1])

            def g_tev(eng, i=i):
                return eng.tensor_copy(gate4[i % 2][:].rearrange("p a b -> p (a b)"),
                                       pT1[0:128, 0:32])
            S.add(f"g_tev_{i}", "vector", g_tev, reads=[pT1], writes=[gate4[i % 2]])

            for s in range(4):
                ti = 4 * i + s
                self._p4a_top2(S, ti, s, gate4[i % 2], r4[ti % 2], me8[ti % 2],
                               slot8[ti % 2], junk8[ti % 2], sct[ti % 2], payl[ti % 2],
                               16 * (ti % 2))

    def _p4a_top2(self, S, ti, s, g4, r4, me8, slot8, junk8, sct, pay2, pdo):
        eio, tokio, ltexf, onesfr = self.eio, self.tokio, self.ltexf, self.onesfr
        onescolf, base8, LISTS = self.onescolf, self.base8, self.LISTS
        pD = self.psum["D"]

        def t_m1(eng):
            return eng.tensor_reduce(r4[:, 0:1], g4[:, s, :], axis=AX.X, op=ALU.max)
        S.add(f"t_m1_{ti}", "vector", t_m1, reads=[g4], writes=[r4])

        def t_o1r(eng):
            return eng.tensor_scalar(r4[:, 8:16], g4[:, s, :], r4[:, 0:1], None, op0=ALU.is_ge)
        S.add(f"t_o1r_{ti}", "vector", t_o1r, reads=[g4, r4], writes=[r4])

        def t_t1(eng):
            return eng.scalar_tensor_tensor(r4[:, 16:24], r4[:, 8:16], -1048576.0, eio[:],
                                            op0=ALU.mult, op1=ALU.add)
        S.add(f"t_t1_{ti}", "vector", t_t1, reads=[r4, eio], writes=[r4])

        def t_e1m(eng):
            return eng.tensor_reduce(r4[:, 5:6], r4[:, 16:24], axis=AX.X, op=ALU.min)
        S.add(f"t_e1m_{ti}", "vector", t_e1m, reads=[r4], writes=[r4])

        def t_oh1(eng):
            return eng.tensor_scalar(r4[:, 8:16], eio[:], r4[:, 5:6], 1048576.0,
                                     op0=ALU.subtract, op1=ALU.is_equal)
        S.add(f"t_oh1_{ti}", "vector", t_oh1, reads=[eio, r4], writes=[r4])

        def t_ec1(eng):
            return eng.tensor_scalar(r4[:, 5:6], r4[:, 5:6], 1048576.0, None, op0=ALU.add)
        S.add(f"t_ec1_{ti}", "vector", t_ec1, reads=[r4], writes=[r4])

        def t_msk(eng):
            return eng.scalar_tensor_tensor(r4[:, 16:24], r4[:, 8:16], -1e9, g4[:, s, :],
                                            op0=ALU.mult, op1=ALU.add)
        S.add(f"t_msk_{ti}", "vector", t_msk, reads=[g4, r4], writes=[r4])

        def t_m2(eng):
            return eng.tensor_reduce(r4[:, 1:2], r4[:, 16:24], axis=AX.X, op=ALU.max)
        S.add(f"t_m2_{ti}", "vector", t_m2, reads=[r4], writes=[r4])

        def t_o2r(eng):
            return eng.tensor_scalar(r4[:, 16:24], r4[:, 16:24], r4[:, 1:2], None, op0=ALU.is_ge)
        S.add(f"t_o2r_{ti}", "vector", t_o2r, reads=[r4], writes=[r4])

        def t_t2(eng):
            return eng.scalar_tensor_tensor(r4[:, 24:32], r4[:, 16:24], -1048576.0, eio[:],
                                            op0=ALU.mult, op1=ALU.add)
        S.add(f"t_t2_{ti}", "vector", t_t2, reads=[r4, eio], writes=[r4])

        def t_e2m(eng):
            return eng.tensor_reduce(r4[:, 6:7], r4[:, 24:32], axis=AX.X, op=ALU.min)
        S.add(f"t_e2m_{ti}", "vector", t_e2m, reads=[r4], writes=[r4])

        def t_oh2(eng):
            return eng.tensor_scalar(r4[:, 16:24], eio[:], r4[:, 6:7], 1048576.0,
                                     op0=ALU.subtract, op1=ALU.is_equal)
        S.add(f"t_oh2_{ti}", "vector", t_oh2, reads=[eio, r4], writes=[r4])

        def t_ec2(eng):
            return eng.tensor_scalar(r4[:, 6:7], r4[:, 6:7], 1048576.0, None, op0=ALU.add)
        S.add(f"t_ec2_{ti}", "vector", t_ec2, reads=[r4], writes=[r4])

        def t_d(eng):
            return eng.tensor_tensor(r4[:, 2:3], r4[:, 0:1], r4[:, 1:2], ALU.subtract)
        S.add(f"t_d_{ti}", "vector", t_d, reads=[r4], writes=[r4])

        def t_p1(eng):
            return eng.activation(r4[:, 3:4], r4[:, 2:3], AF.Sigmoid, bias=0.0, scale=1.0)
        S.add(f"t_p1_{ti}", "scalar", t_p1, reads=[r4], writes=[r4])

        def t_p2(eng):
            return eng.tensor_scalar(r4[:, 4:5], r4[:, 3:4], 1.0, -1.0,
                                     op0=ALU.subtract, op1=ALU.mult)
        S.add(f"t_p2_{ti}", "vector", t_p2, reads=[r4], writes=[r4])

        def t_me(eng):
            return eng.tensor_tensor(me8[:], r4[:, 8:16], r4[:, 16:24], ALU.add)
        S.add(f"t_me_{ti}", "vector", t_me, reads=[r4], writes=[me8])

        def t_lc(eng):
            return eng.matmul(pD[0:128, pdo:pdo + 8], ltexf[:], me8[:], start=True, stop=True)
        S.add(f"t_lc_{ti}", "tensor", t_lc, reads=[ltexf, me8], writes=[pD])

        def t_bb(eng):
            return eng.matmul(pD[0:128, pdo + 8:pdo + 16], onesfr[:], base8[0:1, 0:8],
                              start=True, stop=True)
        S.add(f"t_bb_{ti}", "tensor", t_bb, reads=[onesfr, base8], writes=[pD])

        def t_sl(eng):
            return eng.tensor_copy(slot8[:], pD[0:128, pdo:pdo + 8])
        S.add(f"t_sl_{ti}", "vector", t_sl, reads=[pD], writes=[slot8])

        def t_sl2(eng):
            return eng.tensor_tensor(slot8[:], slot8[:], pD[0:128, pdo + 8:pdo + 16], ALU.add)
        S.add(f"t_sl2_{ti}", "vector", t_sl2, reads=[slot8, pD], writes=[slot8])

        def t_cnt(eng):
            return eng.matmul(pD[0:1, 32 + pdo:40 + pdo], onescolf[:, 0:1], me8[:],
                              start=True, stop=True)
        S.add(f"t_cnt_{ti}", "tensor", t_cnt, reads=[onescolf, me8], writes=[pD])

        def t_bup(eng):
            return eng.tensor_tensor(base8[0:1, 0:8], base8[0:1, 0:8], pD[0:1, 32 + pdo:40 + pdo],
                                     ALU.add)
        S.add(f"t_bup_{ti}", "vector", t_bup, reads=[base8, pD], writes=[base8])

        for j, (ohc, pc, ec) in enumerate(((8, 3, 5), (16, 4, 6))):
            def t_sj(eng, ohc=ohc):
                return eng.scalar_tensor_tensor(junk8[:], r4[:, ohc:ohc + 8], 1.0, slot8[:],
                                                op0=ALU.mult, op1=ALU.mult, accum_out=r4[:, 7:8])
            S.add(f"t_sj_{ti}_{j}", "vector", t_sj, reads=[r4, slot8], writes=[r4, junk8])

            def t_si(eng, ec=ec):
                return eng.tensor_scalar(r4[:, 24:25], r4[:, ec:ec + 1],
                                         8192.0, r4[:, 7:8], op0=ALU.mult, op1=ALU.add)
            S.add(f"t_si_{ti}_{j}", "vector", t_si, reads=[r4], writes=[r4])

            def t_cast(eng, j=j):
                return eng.tensor_copy(sct[:, j:j + 1], r4[:, 24:25])
            S.add(f"t_cast_{ti}_{j}", "vector", t_cast, reads=[r4], writes=[sct])

            def t_se(eng, j=j, ec=ec):
                return eng.tensor_copy(self.SIDXE[:, ti, j:j + 1], r4[:, ec:ec + 1])
            S.add(f"t_se_{ti}_{j}", "vector", t_se, reads=[r4], writes=[self.SIDXE])

            def t_slo(eng, j=j):
                return eng.tensor_copy(self.SIDXS[:, ti, j:j + 1], r4[:, 7:8])
            S.add(f"t_slo_{ti}_{j}", "vector", t_slo, reads=[r4], writes=[self.SIDXS])

            def t_pay1(eng, j=j):
                return eng.tensor_scalar(pay2[:, j, 0:1], tokio[:], float(128 * ti), None,
                                         op0=ALU.add)
            S.add(f"t_pay1_{ti}_{j}", "vector", t_pay1, reads=[tokio], writes=[pay2])

            def t_pay2(eng, j=j, pc=pc):
                return eng.tensor_copy(pay2[:, j, 1:2], r4[:, pc:pc + 1])
            S.add(f"t_pay2_{ti}_{j}", "vector", t_pay2, reads=[r4], writes=[pay2])

        @dma_step(2, grouped=True)
        def t_sc(eng):
            return [eng.indirect_dma_start(
                        out=LISTS, out_offset=IndirectOffsetOnAxis(sct[:, j:j + 1], 0),
                        in_=pay2[:, j, :], in_offset=None) for j in range(2)]
        S.add(f"t_sc_{ti}", "gpsimd", t_sc, reads=[pay2, sct],
              writes=[self._lp[ti % 8]], deps=["z_st"])

    def _bias_col(self, S, tag, src1d, dstcol):
        stg, pE = self.b1stg, self.psum["E"]

        @dma_step(1)
        def bld(eng):
            return eng.dma_start(out=stg[:], in_=src1d.rearrange("(g p) -> g p", g=32))
        S.add(f"{tag}_bld", "sync", bld, writes=[stg])

        def btr(eng):
            return eng.transpose(pE[0:128, 64:96], stg[:], self.identf[0:32, 0:32])
        S.add(f"{tag}_btr", "tensor", btr, reads=[stg, self.identf], writes=[pE])

        def bev(eng):
            return eng.tensor_copy(dstcol[:], pE[0:128, 64:96])
        S.add(f"{tag}_bev", "vector", bev, reads=[pE], writes=[dstcol])

    def _ffn_alloc(self):
        self.preset("W"); self.preset("BF"); self.preset("F32")
        d = {}
        d["w1f"] = [self.palloc("W", [128, 8192], "w1a"),
                    self.palloc("BF", [128, 8192], "w1b")]
        d["w2f"] = [self.palloc("W", [128, 8192], "w2a"),
                    self.palloc("BF", [128, 8192], "w2b")]
        d["w1"] = [self.alias(v.rearrange("p (a b) -> p a b", a=8), v) for v in d["w1f"]]
        d["w2"] = [self.alias(v.rearrange("p (a b) -> p a b", a=8), v) for v in d["w2f"]]
        d["gh"] = self.palloc("BF", [128, 8, 1024], "gh")
        d["yacc"] = self.palloc("F32", [128, 8, 1024], "yacc")
        d["wstg"] = [self.palloc("F32", [128, 4, 1024], f"wstg{i}") for i in range(2)]
        self._ffn_mark_bf = self.pmark("BF")
        self._ffn_mark_w = 16384
        return d

    def _wq_load(self, S, tag, d, W1s, W2s, q, qs):
        w1t, w2t = d["w1"][qs], d["w2"][qs]
        for a in range(2):
            stg = d["wstg"][a]

            @dma_step(1)
            def st1(eng, a=a, stg=stg, W1s=W1s, q=q):
                return eng.dma_start(
                    out=stg[:],
                    in_=W1s[512 * a:512 * (a + 1), 1024 * q:1024 * (q + 1)]
                        .rearrange("(c p) j -> p c j", p=128))
            S.add(f"{tag}_s1_{a}", "sync", st1, reads=[W1s], writes=[stg])

            def c1(eng, a=a, stg=stg, w1t=w1t):
                return eng.activation(w1t[:, 4 * a:4 * (a + 1), :], stg[:],
                                      AF.Copy, bias=0.0, scale=1.0)
            S.add(f"{tag}_c1_{a}", "scalar", c1, reads=[stg], writes=[w1t])
        for a in range(2):
            stg = d["wstg"][a]

            @dma_step(1)
            def st2(eng, a=a, stg=stg, W2s=W2s, q=q):
                return eng.dma_start(
                    out=stg[:],
                    in_=W2s[1024 * q + 512 * a:1024 * q + 512 * (a + 1), :]
                        .rearrange("(c p) j -> p c j", p=128))
            S.add(f"{tag}_s2_{a}", "gpsimd", st2, reads=[W2s], writes=[stg])

            def c2(eng, a=a, stg=stg, w2t=w2t):
                return eng.tensor_copy(w2t[:, 4 * a:4 * (a + 1), :]
                                       .rearrange("p a b -> p (a b)"),
                                       stg[:].rearrange("p a b -> p (a b)"))
            S.add(f"{tag}_c2_{a}", "vector", c2, reads=[stg], writes=[w2t])

    def _ffn_quarter(self, S, tag, d, qs, q, xT, b1c, bcol0, b2form, first_acc):
        w1t, w2t, gh, yacc = d["w1"][qs], d["w2"][qs], d["gh"], d["yacc"]
        pA, pB, pC, pD = (self.psum[k] for k in ("A", "B", "C", "D"))
        onesrow = self.onesrow
        for nt in range(2):
            for jc in range(8):
                pp = (pA, pB)[(nt * 8 + jc) % 2]

                def l1(eng, jc=jc, nt=nt, pp=pp, w1t=w1t, xT=xT):
                    ii = None
                    for kc in range(8):
                        ii = eng.matmul(pp[:], w1t[:, kc, 128 * jc:128 * (jc + 1)],
                                        xT[:, kc, 512 * nt:512 * (nt + 1)],
                                        start=(kc == 0), stop=(kc == 7))
                    return ii
                S.add(f"{tag}_l1_{nt}_{jc}", "tensor", l1, reads=[w1t, xT], writes=[pp])

                def ge(eng, jc=jc, nt=nt, pp=pp, gh=gh, b1c=b1c, g=bcol0 + jc):
                    return eng.activation(gh[:, jc, 512 * nt:512 * (nt + 1)], pp[:],
                                          AF.Gelu, bias=b1c[:, g:g + 1], scale=1.0)
                S.add(f"{tag}_ge_{nt}_{jc}", "scalar", ge, reads=[pp, b1c], writes=[gh])
        for sub in range(8):
            for half in range(2):
                pp = (pC, pD)[(sub * 2 + half) % 2]

                def l2(eng, sub=sub, half=half, pp=pp, gh=gh, w2t=w2t, q=q, b2form=b2form):
                    ii = None
                    for jc in range(8):
                        ii = eng.matmul(pp[:], gh[:, jc, 128 * sub:128 * (sub + 1)],
                                        w2t[:, jc, 512 * half:512 * (half + 1)],
                                        start=(jc == 0), stop=(jc == 7 and q != 0))
                    if q == 0:
                        if b2form[0] == "row":
                            ii = eng.matmul(pp[:], onesrow[0:1, 0:128],
                                            b2form[1][0:1, 512 * half:512 * (half + 1)],
                                            start=False, stop=True)
                        else:
                            ii = eng.matmul(pp[:], b2form[1][0:8, 0:128],
                                            b2form[2][0:8, 512 * half:512 * (half + 1)],
                                            start=False, stop=True)
                    return ii
                rds = [gh, w2t, onesrow] + list(b2form[1:]) if q == 0 else [gh, w2t]
                S.add(f"{tag}_l2_{sub}_{half}", "tensor", l2, reads=rds, writes=[pp])

                if first_acc:
                    def ev(eng, sub=sub, half=half, pp=pp, yacc=yacc):
                        return eng.tensor_copy(yacc[:, sub, 512 * half:512 * (half + 1)], pp[:])
                else:
                    def ev(eng, sub=sub, half=half, pp=pp, yacc=yacc):
                        return eng.tensor_tensor(yacc[:, sub, 512 * half:512 * (half + 1)],
                                                 yacc[:, sub, 512 * half:512 * (half + 1)],
                                                 pp[:], ALU.add)
                S.add(f"{tag}_ev_{sub}_{half}", "vector", ev, reads=[pp, yacc], writes=[yacc])


    def emit_p4sched(self):
        S = self.sched
        eio, tokio, junk8 = self.eio, self.tokio, self.junk8
        ltexf, identf8, base8 = self.ltexf, self.identf8, self.base8
        onesfr, pE = self.onesfr, self.psum["E"]
        ccol = self.sb("ccol", [128, 1], F32)
        ntcol = self.sb("ntcol", [128, 1], F32)
        tbcol = self.sb("tbcol", [128, 1], F32)
        TBrow = self.sb("TBrow", [1, 8], F32)
        TBb = self.sb("TBb", [128, 8], F32)
        PBTb = self.sb("PBTb", [128, 8], F32)
        exb = self.sb("exb", [128, 32], F32)
        tio32 = self.sb("tio32", [128, 32], F32)
        swk32 = self.sb("swk32", [128, 32], F32)
        swk8 = self.sb("swk8", [128, 8], F32)
        swkc = self.sb("swkc", [128, 4], F32)
        self.llidxI = self.sb("llidxI", [128, NT], I32)
        self.w1iI = self.sb("w1iI", [128, NT], I32)
        self.b1iI = self.sb("b1iI", [128, NT], I32)
        self.ohcall = self.sb("ohcall", [128, NT], F32)
        SIDX, SIDXE, SIDXS = self.SIDX, self.SIDXE, self.SIDXS

        @dma_step(1)
        def sc_ld(eng):
            return eng.dma_start(out=tio32[:], in_=self.cst["tiota32"])
        S.add("sc_ld", "sync", sc_ld, writes=[tio32])

        def sc_cmm(eng):
            return eng.matmul(pE[0:8, 96:97], base8[0:1, 0:8], onesfr[0:1, 0:1],
                              start=True, stop=True)
        S.add("sc_cmm", "tensor", sc_cmm, reads=[base8, onesfr], writes=[pE])

        def sc_cev(eng):
            return eng.tensor_copy(ccol[0:8, 0:1], pE[0:8, 96:97])
        S.add("sc_cev", "vector", sc_cev, reads=[pE], writes=[ccol])

        def sc_thr(eng):
            return eng.tensor_scalar(swk8[:], tio32[:, 0:8], 1024.0, None, op0=ALU.mult)
        S.add("sc_thr", "vector", sc_thr, reads=[tio32], writes=[swk8])

        def sc_cmp(eng):
            return eng.tensor_scalar(swk32[0:8, 0:8], swk8[0:8, :], ccol[0:8, 0:1], None, op0=ALU.is_lt)
        S.add("sc_cmp", "vector", sc_cmp, reads=[swk8, ccol], writes=[swk32])

        def sc_nt(eng):
            return eng.tensor_reduce(ntcol[0:8, 0:1], swk32[0:8, 0:8], axis=AX.X, op=ALU.add)
        S.add("sc_nt", "vector", sc_nt, reads=[swk32], writes=[ntcol])

        def sc_tbm(eng):
            return eng.matmul(pE[0:8, 97:98], ltexf[0:8, 0:8], ntcol[0:8, 0:1],
                              start=True, stop=True)
        S.add("sc_tbm", "tensor", sc_tbm, reads=[ltexf, ntcol], writes=[pE])

        def sc_tbe(eng):
            return eng.tensor_copy(tbcol[0:8, 0:1], pE[0:8, 97:98])
        S.add("sc_tbe", "vector", sc_tbe, reads=[pE], writes=[tbcol])

        def sc_tbr(eng):
            return eng.matmul(pE[0:1, 104:112], tbcol[0:8, 0:1], identf8[:],
                              start=True, stop=True)
        S.add("sc_tbr", "tensor", sc_tbr, reads=[tbcol, identf8], writes=[pE])

        def sc_tbrev(eng):
            return eng.tensor_copy(TBrow[:], pE[0:1, 104:112])
        S.add("sc_tbrev", "vector", sc_tbrev, reads=[pE], writes=[TBrow])

        def sc_tbb(eng):
            return eng.matmul(pE[0:128, 112:120], onesfr[:], TBrow[:], start=True, stop=True)
        S.add("sc_tbb", "tensor", sc_tbb, reads=[onesfr, TBrow], writes=[pE])

        def sc_tbbe(eng):
            return eng.tensor_copy(TBb[:], pE[0:128, 112:120])
        S.add("sc_tbbe", "vector", sc_tbbe, reads=[pE], writes=[TBb])

        def sc_pbt(eng):
            return eng.tensor_scalar(PBTb[:], TBb[:], 1024.0, None, op0=ALU.mult)
        S.add("sc_pbt", "vector", sc_pbt, reads=[TBb], writes=[PBTb])

        for e in range(NE):
            if e == 0:
                def sc_ex(eng, e=e):
                    return eng.tensor_scalar(exb[:, 0:NT], tio32[:, 0:NT], TBb[:, e:e + 1],
                                             None, op0=ALU.is_ge)
                S.add(f"sc_ex_{e}", "vector", sc_ex, reads=[tio32, TBb], writes=[exb])
            else:
                def sc_ex1(eng, e=e):
                    return eng.tensor_scalar(swk32[:, 0:NT], tio32[:, 0:NT], TBb[:, e:e + 1],
                                             None, op0=ALU.is_ge)
                S.add(f"sc_ex1_{e}", "vector", sc_ex1, reads=[tio32, TBb], writes=[swk32])

                def sc_ex2(eng, e=e):
                    return eng.tensor_tensor(exb[:, 0:NT], exb[:, 0:NT], swk32[:, 0:NT], ALU.add)
                S.add(f"sc_ex2_{e}", "vector", sc_ex2, reads=[exb, swk32], writes=[exb])

        def sc_exm1(eng):
            return eng.tensor_scalar(exb[:, 0:NT], exb[:, 0:NT], 1.0, None, op0=ALU.subtract)
        S.add("sc_exm1", "vector", sc_exm1, reads=[exb], writes=[exb])

        for t in range(NT):
            exc = exb[:, t:t + 1]

            def st_w1f(eng, t=t, exc=exc):
                return eng.tensor_scalar(swkc[:, 0:1], exc, 512.0, tokio[:, 0:1],
                                         op0=ALU.mult, op1=ALU.add)
            S.add(f"st_w1f_{t}", "vector", st_w1f, reads=[exb, tokio], writes=[swkc])

            def st_w1c(eng, t=t):
                return eng.tensor_copy(self.w1iI[:, t:t + 1], swkc[:, 0:1])
            S.add(f"st_w1c_{t}", "vector", st_w1c, reads=[swkc], writes=[self.w1iI])

            def st_oh(eng, t=t, exc=exc):
                return eng.tensor_scalar(swk8[:], eio[:], exc, None, op0=ALU.is_equal)
            S.add(f"st_oh_{t}", "vector", st_oh, reads=[eio, exb], writes=[swk8])

            def st_ohc(eng, t=t, exc=exc):
                return eng.tensor_scalar(self.ohcall[:, t:t + 1], tokio[:], exc, None, op0=ALU.is_equal)
            S.add(f"st_ohc_{t}", "vector", st_ohc, reads=[tokio, exb], writes=[self.ohcall])

            def st_tbs(eng, t=t):
                return eng.scalar_tensor_tensor(junk8[:], swk8[:], 1.0, TBb[:],
                                                op0=ALU.mult, op1=ALU.mult, accum_out=swkc[:, 1:2])
            S.add(f"st_tbs_{t}", "vector", st_tbs, reads=[swk8, TBb], writes=[junk8, swkc])

            def st_ltb(eng, t=t):
                return eng.tensor_scalar(swkc[:, 1:2], swkc[:, 1:2], -1024.0, 1024.0 * t,
                                         op0=ALU.mult, op1=ALU.add)
            S.add(f"st_ltb_{t}", "vector", st_ltb, reads=[swkc], writes=[swkc])

            def st_e8k(eng, t=t, exc=exc):
                return eng.tensor_scalar(swkc[:, 2:3], exc, 8192.0, swkc[:, 1:2],
                                         op0=ALU.mult, op1=ALU.add)
            S.add(f"st_e8k_{t}", "vector", st_e8k, reads=[exb, swkc], writes=[swkc])

            def st_clmp(eng, t=t):
                return eng.tensor_scalar(swkc[:, 2:3], swkc[:, 2:3], float(NE * 8192 - 1024),
                                         None, op0=ALU.min)
            S.add(f"st_clmp_{t}", "vector", st_clmp, reads=[swkc], writes=[swkc])

            def st_llf(eng, t=t):
                return eng.tensor_tensor(swkc[:, 2:3], swkc[:, 2:3], tokio[:, 0:1], ALU.add)
            S.add(f"st_llf_{t}", "vector", st_llf, reads=[swkc, tokio], writes=[swkc])

            def st_llc(eng, t=t):
                return eng.tensor_copy(self.llidxI[:, t:t + 1], swkc[:, 2:3])
            S.add(f"st_llc_{t}", "vector", st_llc, reads=[swkc], writes=[self.llidxI])

            def st_b1f(eng, t=t, exc=exc):
                return eng.tensor_scalar(swkc[:, 3:4], exc, 32.0, None, op0=ALU.mult)
            S.add(f"st_b1f_{t}", "vector", st_b1f, reads=[exb], writes=[swkc])

            def st_b1a(eng, t=t):
                return eng.tensor_tensor(swkc[:, 3:4], swkc[:, 3:4], tokio[:], ALU.add)
            S.add(f"st_b1a_{t}", "vector", st_b1a, reads=[swkc, tokio], writes=[swkc])

            def st_b1c(eng, t=t):
                return eng.tensor_copy(self.b1iI[:, t:t + 1], swkc[:, 3:4])
            S.add(f"st_b1c_{t}", "vector", st_b1c, reads=[swkc], writes=[self.b1iI])

        for ti in range(64):
            for j in range(2):
                def sx_oh(eng, ti=ti, j=j):
                    return eng.tensor_scalar(swk8[:], eio[:], SIDXE[:, ti, j:j + 1],
                                             None, op0=ALU.is_equal)
                S.add(f"sx_oh_{ti}_{j}", "vector", sx_oh, reads=[eio, SIDXE], writes=[swk8])

                def sx_pb(eng, ti=ti, j=j):
                    return eng.scalar_tensor_tensor(junk8[:], swk8[:], 1.0, PBTb[:],
                                                    op0=ALU.mult, op1=ALU.mult,
                                                    accum_out=swkc[:, 0:1])
                S.add(f"sx_pb_{ti}_{j}", "vector", sx_pb, reads=[swk8, PBTb], writes=[junk8, swkc])

                def sx_add(eng, ti=ti, j=j):
                    return eng.tensor_tensor(swkc[:, 1:2], swkc[:, 0:1], SIDXS[:, ti, j:j + 1], ALU.add)
                S.add(f"sx_add_{ti}_{j}", "vector", sx_add, reads=[swkc, SIDXS], writes=[swkc])

                def sx_c(eng, ti=ti, j=j):
                    return eng.tensor_copy(SIDX[:, ti, j:j + 1], swkc[:, 1:2])
                S.add(f"sx_c_{ti}_{j}", "vector", sx_c, reads=[swkc], writes=[SIDX])

    def emit_wprep(self, lo, hi):
        S = self.sched
        for i in range(lo, hi):
            if i < 32:
                e, q = i // 4, i % 4

                @dma_step(1, grouped=True)
                def wp(eng, e=e, q=q):
                    return eng.dma_start(
                        out=self.WB1R[(e * 4 + q) * 128:(e * 4 + q) * 128 + 128, :]
                            .rearrange("p (kc j) -> p kc j", kc=8),
                        in_=self.W1[e, :, 1024 * q:1024 * (q + 1)]
                            .rearrange("(kc p) j -> p kc j", p=128))
                S.add(f"wp1_{e}_{q}", "gpsimd", wp, reads=[self.W1], writes=[self.WB1R])
            else:
                e, q = (i - 32) // 4, (i - 32) % 4

                @dma_step(1, grouped=True)
                def wp(eng, e=e, q=q):
                    return eng.dma_start(
                        out=self.WB2R[(e * 4 + q) * 128:(e * 4 + q) * 128 + 128, :]
                            .rearrange("p (jc d) -> p jc d", jc=8),
                        in_=self.W2[e, 1024 * q:1024 * (q + 1), :]
                            .rearrange("(jc p) d -> p jc d", p=128))
                S.add(f"wp2_{e}_{q}", "gpsimd", wp, reads=[self.W2], writes=[self.WB2R])
    def emit_p4b(self, interleave=None):
        S = self.sched
        d = self._ffn_alloc()
        self._ffnbufs = d
        xfTl = [self.palloc("BF", [128, 8, 1024], "xfTl0"),
                self.palloc("W", [128, 8, 1024], "xfTl1")]
        b1c = [self.sb(f"shb1c{s}", [128, 32], F32) for s in range(2)]
        b2r = [self.bkrow, self.bvrow]
        XFT, SH = self.XFT, self.SH

        for s in range(2):
            self._bias_col(S, f"p4b_b1_{s}", self.sb1[s], b1c[s])

            @dma_step(1)
            def bl2(eng, s=s):
                return eng.dma_start(out=b2r[s][:],
                                     in_=self.sb2[s].rearrange("(b a) -> b a", b=1))
            S.add(f"p4b_bl2_{s}", "gpsimd", bl2, reads=[self.sb2], writes=[b2r[s]])

        for ti in range(8):
            xT = xfTl[ti % 2]

            @dma_step(1)
            def xl(eng, ti=ti, xT=xT):
                return eng.dma_start(out=xT[:],
                                     in_=XFT[:, 1024 * ti:1024 * (ti + 1)]
                                         .rearrange("(c p) t -> p c t", p=128))
            S.add(f"p4b_xl_{ti}", "scalar", xl, reads=[XFT], writes=[xT])

            for s in range(2):
                for q in range(4):
                    qs = (s * 4 + q) % 2
                    self._wq_load(S, f"p4b_w_{ti}_{s}_{q}", d, self.sW1[s], self.sW2[s], q, qs)
                    self._ffn_quarter(S, f"p4b_f_{ti}_{s}_{q}", d, qs, q, xT,
                                      b1c[s], 8 * q, ("row", b2r[s]),
                                      first_acc=(s == 0 and q == 0))

            @dma_step(1)
            def shst(eng, ti=ti):
                return eng.dma_start(out=SH[1024 * ti:1024 * (ti + 1), :]
                                     .rearrange("(s p) d -> p s d", p=128),
                                     in_=d["yacc"][:])
            S.add(f"p4b_shst_{ti}", "sync", shst, reads=[d["yacc"]], writes=[SH])
            if ti == 0 and interleave is not None:
                interleave()

    def emit_p5(self):
        S = self.sched
        d = self._ffnbufs
        self.preset_to("BF", self._ffn_mark_bf)
        self.preset_to("W", self._ffn_mark_w)
        xgTf = self.palloc("BF", [128, 8192], "xgT")
        xgT = self.alias(xgTf.rearrange("p (a b) -> p a b", a=8), xgTf)
        xge = [self.palloc("W", [128, 1024], f"xge{i}") for i in range(2)]
        ysb = [self.palloc("W", [128, 1024], f"ysb{i}") for i in range(2)]
        lle = self.sb("lle", [128, 8, 2], F32)
        tokidx = self.sb("tokidx", [128, 8], I32)
        gb1c = self.sb("gb1c", [128, 32], F32)
        b2allb = self.palloc("W", [8, 1024], "b2allb")
        ohbt = [self.palloc("W", [8, 128], f"ohbt{i}") for i in range(2)]
        pT1, pT2, pE = self.psum["T1"], self.psum["T2"], self.psum["E"]
        XF, YS, LISTS = self.XF, self.YS, self.LISTS
        WB1R, WB2R = self.WB1R, self.WB2R
        ident = self.ident
        b1v = self.b1.rearrange("e (g p) -> (e g) p", p=128)

        @dma_step(1)
        def b2ld(eng):
            return eng.dma_start(out=b2allb[:], in_=self.b2)
        S.add("p5_b2ld", "gpsimd", b2ld, reads=[self.b2], writes=[b2allb])

        for t in range(NT):
            par = t % 2

            for c in range(8):
                @dma_step(1, grouped=True)
                def lld(eng, t=t, c=c):
                    return eng.indirect_dma_start(
                        out=lle[:, c, :], out_offset=None, in_=LISTS,
                        in_offset=IndirectOffsetOnAxis(self.llidxI[:, t:t + 1], 0),
                        element_offset=256 * c)
                S.add(f"p5_lld_{t}_{c}", "gpsimd", lld,
                      reads=[LISTS] + self._lp + [self.llidxI], writes=[lle])

            def tcast(eng, t=t):
                return eng.tensor_copy(tokidx[:],
                                       lle[:, :, 0:1].rearrange("p a b -> p (a b)"))
            S.add(f"p5_tcast_{t}", "vector", tcast, reads=[lle], writes=[tokidx])

            @dma_step(1)
            def b1g(eng, t=t):
                return eng.indirect_dma_start(
                    out=self.b1stg[:], out_offset=None, in_=b1v,
                    in_offset=IndirectOffsetOnAxis(self.b1iI[0:32, t:t + 1], 0))
            S.add(f"p5_b1g_{t}", "gpsimd", b1g, reads=[self.b1, self.b1iI], writes=[self.b1stg])

            def b1tr(eng, t=t):
                return eng.transpose(pE[0:128, 64:96], self.b1stg[:], self.identf[0:32, 0:32])
            S.add(f"p5_b1tr_{t}", "tensor", b1tr, reads=[self.b1stg, self.identf], writes=[pE])

            def b1ev(eng, t=t):
                return eng.tensor_copy(gb1c[:], pE[0:128, 64:96])
            S.add(f"p5_b1ev_{t}", "vector", b1ev, reads=[pE], writes=[gb1c])

            def ohmk(eng, t=t, par=par):
                return eng.tensor_scalar(ohbt[par][:], self.onesbb[0:8, 0:128],
                                         self.ohcall[0:8, t:t + 1], None, op0=ALU.mult)
            S.add(f"p5_ohmk_{t}", "vector", ohmk,
                  reads=[self.onesbb, self.ohcall], writes=[ohbt[par]])

            for c in range(8):
                xb = xge[c % 2]

                @dma_step(1, grouped=True)
                def xg(eng, t=t, c=c, xb=xb):
                    return eng.indirect_dma_start(
                        out=xb[:], out_offset=None, in_=XF,
                        in_offset=IndirectOffsetOnAxis(tokidx[:, c:c + 1], 0))
                S.add(f"p5_xg_{t}_{c}", "gpsimd", xg, reads=[XF, tokidx], writes=[xb])

                for g4 in range(2):
                    ptr = (pT1, pT2)[g4]

                    def xtr(eng, xb=xb, g4=g4, ptr=ptr):
                        ii = None
                        for a in range(4):
                            kc = 4 * g4 + a
                            ii = eng.transpose(ptr[:, 128 * a:128 * (a + 1)],
                                               xb[:, 128 * kc:128 * (kc + 1)], ident[:])
                        return ii
                    S.add(f"p5_xtr_{t}_{c}_{g4}", "tensor", xtr, reads=[xb, ident], writes=[ptr])

                    def xev(eng, c=c, g4=g4, ptr=ptr):
                        return eng.activation(
                            xgT[:, 4 * g4:4 * (g4 + 1), 128 * c:128 * (c + 1)],
                            ptr[:].rearrange("p (a q) -> p a q", a=4),
                            AF.Copy, bias=0.0, scale=1.0)
                    S.add(f"p5_xev_{t}_{c}_{g4}", "scalar", xev, reads=[ptr], writes=[xgT])

            for q in range(4):
                qs = (t * 4 + q) % 2

                @dma_step(1, grouped=True)
                def w1g(eng, t=t, q=q, qs=qs):
                    return eng.indirect_dma_start(
                        out=d["w1f"][qs][:], out_offset=None, in_=WB1R,
                        in_offset=IndirectOffsetOnAxis(self.w1iI[:, t:t + 1], 0),
                        element_offset=1048576 * q)
                S.add(f"p5_w1g_{t}_{q}", "gpsimd", w1g,
                      reads=[WB1R, self.w1iI], writes=[d["w1f"][qs]])

                @dma_step(1, grouped=True)
                def w2g(eng, t=t, q=q, qs=qs):
                    return eng.indirect_dma_start(
                        out=d["w2f"][qs][:], out_offset=None, in_=WB2R,
                        in_offset=IndirectOffsetOnAxis(self.w1iI[:, t:t + 1], 0),
                        element_offset=1048576 * q)
                S.add(f"p5_w2g_{t}_{q}", "gpsimd", w2g,
                      reads=[WB2R, self.w1iI], writes=[d["w2f"][qs]])

                self._ffn_quarter(S, f"p5_f_{t}_{q}", d, qs, q, xgT,
                                  gb1c, 8 * q, ("oh", ohbt[par], b2allb),
                                  first_acc=(q == 0))

            for sub in range(8):
                yb = ysb[sub % 2]

                def gs(eng, sub=sub, yb=yb):
                    return eng.tensor_scalar(yb[:], d["yacc"][:, sub, :],
                                             lle[:, sub, 1:2], None, op0=ALU.mult)
                S.add(f"p5_gs_{t}_{sub}", "vector", gs,
                      reads=[d["yacc"], lle], writes=[yb])

                @dma_step(1, grouped=True)
                def yst(eng, t=t, sub=sub, yb=yb):
                    r0 = 1024 * t + 128 * sub
                    return eng.dma_start(out=YS[r0:r0 + 128, :], in_=yb[:])
                S.add(f"p5_yst_{t}_{sub}", "sync", yst, reads=[yb], writes=[YS])

    def emit_p6(self):
        S = self.sched
        self.preset("W"); self.preset("BF"); self.preset("F32")
        x1p = [self.palloc("F32", [128, 1024], f"p6x{i}") for i in range(2)]
        shp = [self.palloc("F32", [128, 1024], f"p6s{i}") for i in range(2)]
        ffnb = [self.palloc("F32", [128, 1024], f"p6f{i}") for i in range(2)]
        outp = [self.palloc("F32", [128, 1024], f"p6o{i}") for i in range(2)]
        yg = [[self.palloc("BF", [128, 1024], f"p6y{j}_{i}") for i in range(2)] for j in range(2)]
        junk6 = [self.palloc("BF", [128, 1024], f"junk6{i}") for i in range(2)]
        st6 = [self.sb(f"st6{i}", [128, 16], F32) for i in range(2)]
        X1, SH, YS, OUT, SIDX = self.X1, self.SH, self.YS, self.OUT, self.SIDX

        for i in range(64):
            b = i % 2

            @dma_step(1, grouped=True)
            def x1l(eng, i=i, b=b):
                return eng.dma_start(out=x1p[b][:], in_=X1[128 * i:128 * (i + 1), :])
            S.add(f"p6x1_{i}", "sync", x1l, reads=[X1], writes=[x1p[b]])

            @dma_step(1, grouped=True)
            def shl(eng, i=i, b=b):
                return eng.dma_start(out=shp[b][:], in_=SH[128 * i:128 * (i + 1), :])
            S.add(f"p6sh_{i}", "scalar", shl, reads=[SH], writes=[shp[b]])

            for j in range(2):
                @dma_step(1, grouped=True)
                def ygath(eng, i=i, j=j, b=b):
                    return eng.indirect_dma_start(
                        out=yg[j][b][:], out_offset=None, in_=YS,
                        in_offset=IndirectOffsetOnAxis(SIDX[:, i, j:j + 1], 0))
                S.add(f"p6yg_{i}_{j}", "gpsimd", ygath, reads=[YS, SIDX], writes=[yg[j][b]])

            def yadd(eng, b=b):
                return eng.tensor_tensor(ffnb[b][:], yg[0][b][:], yg[1][b][:], ALU.add)
            S.add(f"p6ya_{i}", "vector", yadd, reads=[yg[0][b], yg[1][b]], writes=[ffnb[b]])

            def shadd(eng, b=b):
                return eng.scalar_tensor_tensor(ffnb[b][:], shp[b][:], 0.5, ffnb[b][:],
                                                op0=ALU.mult, op1=ALU.add)
            S.add(f"p6sa_{i}", "vector", shadd, reads=[shp[b], ffnb[b]], writes=[ffnb[b]])

            self._ln_steps(S, "p6ln", i, st6[b], junk6[b], ffnb[b][:], ffnb[b], outp[b], x1p[b], 0)

            @dma_step(1, grouped=True)
            def ost(eng, i=i, b=b):
                return eng.dma_start(out=OUT[128 * i:128 * (i + 1), :], in_=outp[b][:])
            S.add(f"p6st_{i}", "sync", ost, reads=[outp[b]], writes=[OUT])


def _build_program():
    p = P(debug_outputs=set())
    p.declare_common()
    p.declare_p1()
    p.declare_p2()
    p.declare_p36()
    p.emit_consts()
    p.emit_p1()
    p.emit_p2_sel()
    p.emit_p2_heads(range(H), interleave=lambda h: p.emit_wprep(4 * h, 4 * (h + 1)))
    p.emit_p3()
    p.emit_p4a()
    p.emit_p4sched()
    p.emit_p4b()
    p.emit_p5()
    p.emit_p6()
    with p.ctx:
        with p.nc.Block() as block:
            p.sched.emit(p, block)
    return p


def kernel(**inputs):
    from concourse.bass_utils import run_bass_kernel_spmd
    p = _build_program()
    consts = p.const_inputs()
    shared = dict(
        Wq=inputs["L0_attn_Wq"], bq=inputs["L0_attn_bq"],
        Wk=inputs["L0_attn_Wk"], bk=inputs["L0_attn_bk"],
        Wv=inputs["L0_attn_Wv"], bv=inputs["L0_attn_bv"],
        rW=inputs["L0_ffn_rW"], rb=inputs["L0_ffn_rb"],
        W1=inputs["L0_ffn_W1"], b1=inputs["L0_ffn_b1"],
        W2=inputs["L0_ffn_W2"], b2=inputs["L0_ffn_b2"],
        sW1=inputs["L0_ffn_sW1"], sb1=inputs["L0_ffn_sb1"],
        sW2=inputs["L0_ffn_sW2"], sb2=inputs["L0_ffn_sb2"],
    )
    shared.update(consts)
    X = np.ascontiguousarray(np.asarray(inputs["L0_attn_X"], np.float32))
    in_maps = []
    for core in range(8):
        m = dict(shared)
        m["X"] = X[core]
        in_maps.append(m)
    res = run_bass_kernel_spmd(p.nc, in_maps, core_ids=list(range(8)))
    out = np.stack([np.asarray(res.results[c]["OUT"], np.float32) for c in range(8)])
    return out
